# version 06
# baseline (speedup 1.0000x reference)
import numpy as np
from dataclasses import dataclass, field
from contextlib import ExitStack
import concourse.bass as bass
import concourse.mybir as mybir
from bass_rust import AP
from concourse.bass_utils import run_bass_kernel_spmd


from dataclasses import dataclass, field

@dataclass
class Item:
    idx: int
    eng: str
    fn: object
    reads: tuple
    writes: tuple
    stream: str
    slot: int
    seq: int
    tag: str
    deps: set = field(default_factory=set)
    rank: int = 0

class Tracked:
    def __init__(self, dma_streams=None):
        self.items = []
        self.last_w = {}
        self.readers = {}
        self.streams = dict(dma_streams or {})
        self.stream_items = {s: [] for s in self.streams}

    def sem_names(self):
        names = ["tensor", "vector", "scalar", "gpsimd"]
        for s, (eng, depth) in self.streams.items():
            names += [f"dma_{s}_{j}" for j in range(depth)]
        return names

    def op(self, eng, fn, reads=(), writes=(), dma=None, tag=""):
        idx = len(self.items)
        deps = set()
        for t in reads:
            if t in self.last_w:
                deps.add(self.last_w[t])
        for t in writes:
            if t in self.last_w:
                deps.add(self.last_w[t])
            deps |= self.readers.get(t, set())
        deps.discard(idx)
        slot = seq = 0
        if dma:
            qeng, depth = self.streams[dma]
            assert eng == qeng, f"stream {dma} belongs to queue {qeng}"
            lst = self.stream_items[dma]
            seq = len(lst)
            slot = seq % depth
            if seq >= depth:
                deps.add(lst[seq - depth])
            lst.append(idx)
        it = Item(idx, eng, fn, tuple(reads), tuple(writes),
                  dma or '', slot, seq, tag, deps)
        self.items.append(it)
        for t in reads:
            self.readers.setdefault(t, set()).add(idx)
        for t in writes:
            self.last_w[t] = idx
            self.readers[t] = set()
        return idx

    def _semkey(self, it):
        return f"dma_{it.stream}_{it.slot}" if it.stream else it.eng

    def _assign_ranks(self, needed):
        self.counts = {}
        for it in self.items:
            key = self._semkey(it)
            it.rank = 0
            if it.stream:
                depth = self.streams[it.stream][1]
                it.rank = 16 * (it.seq // depth + 1)
                self.counts[key] = max(self.counts.get(key, 0), it.rank)
            elif it.idx in needed:
                self.counts[key] = self.counts.get(key, 0) + 1
                it.rank = self.counts[key]

    def _plan_waits(self):
        clock_of = {}
        rc = {}
        by_rank = {}
        waits = {}
        used = set()

        def merge(dst, src):
            for k, v in src.items():
                if dst.get(k, 0) < v:
                    dst[k] = v

        for it in self.items:
            eng_rc = rc.setdefault(it.eng, {})
            tgt = {}
            for di in it.deps:
                dit = self.items[di]
                if dit.rank <= 0:
                    continue
                key = self._semkey(dit)
                if tgt.get(key, (0, None))[0] < dit.rank:
                    tgt[key] = (dit.rank, di)
            w = []
            for key, (v, di) in sorted(tgt.items()):
                if eng_rc.get(key, 0) >= v:
                    continue
                w.append((key, v))
                used.add(di)
                merge(eng_rc, clock_of.get(di, {}))
                if eng_rc.get(key, 0) < v:
                    eng_rc[key] = v
            waits[it.idx] = w
            if it.rank:
                key = self._semkey(it)
                c = dict(eng_rc)
                c[key] = max(c.get(key, 0), it.rank)
                clock_of[it.idx] = c
        return waits, used

    def prepare(self, skip_same_engine=("tensor",), verbose=False):
        for it in self.items:
            it.deps = {d for d in it.deps
                       if not (self.items[d].eng == it.eng
                               and not self.items[d].stream and not it.stream
                               and it.eng in skip_same_engine)}
        needed = {d for it in self.items for d in it.deps}
        self._assign_ranks(needed)
        w1, used = self._plan_waits()
        needed2 = {i for i in used if not self.items[i].stream}
        needed2 |= {i for i in needed if self.items[i].stream}
        self._assign_ranks(needed2)
        self.waits, used2 = self._plan_waits()
        self.nwait = sum(len(v) for v in self.waits.values())
        if verbose:
            ninc = sum(1 for it in self.items if it.rank and not it.stream)
            print(f"[trk] items={len(self.items)} incs={ninc} waits={self.nwait} "
                  f"dmas={sum(1 for it in self.items if it.stream)}")

    def emit_engine(self, name, eng_obj, sems, final_waiter=False):
        seen = {}
        for it in self.items:
            if it.eng != name:
                continue
            for key, v in self.waits.get(it.idx, ()):
                eng_obj.wait_ge(sems[key], v)
                seen[key] = max(seen.get(key, 0), v)
            inst = it.fn(eng_obj)
            if it.rank:
                assert inst is not None, f"needs inc but fn returned None: {it.tag}"
                inst.then_inc(sems[self._semkey(it)], 16 if it.stream else 1)
        if final_waiter:
            for key, v in sorted(self.counts.items()):
                if key.startswith('dma_') and seen.get(key, 0) < v:
                    eng_obj.wait_ge(sems[key], v)



"""Megakernel builder for mk_sandwich_d1 (linear_attn sandwich block, depth 1).

Per core: one batch element, seq = NG*256 tokens, d_model 1024.
Token-major residual stream; dm-major (transposed) activations feed TensorE.

Layout conventions (tok = c*128 + p within a group of 256):
  x/y/ao/t35 buffers: (128, 2048) f32 viewed per tok-chunk c as [:, c*1024:(c+1)*1024]
  t_bf/xln_bf:        (128, 2048) bf16 same
  tT/xlnT:            (128, 8, 256) bf16  [dm-chunk k on dim1, tok on dim2]
  v_bf:               (128, 2, 2048) bf16 [tok-chunk c on dim1, hid e on dim2]
  gateT:              (128, 16, 256) bf16 [e-chunk on dim1, tok on dim2]
  Wh_bf:              (128, 8, 4096) bf16 [dm-chunk k, hid-out h]
  Wqk_bf:             (128, 8, 128)  bf16
  sWT_bf:             (128, 8, 1024) bf16 [i-chunk r, out o]
  S:                  (128, 2048) f32 (qk-dim d on partitions, hid e free)
PSUM banks: pA(0-1), pB(2), pC(3-4), pD(5-6), pE(7)
"""

F32 = mybir.dt.float32
BF16 = mybir.dt.bfloat16
AF = mybir.ActivationFunctionType
ALU = mybir.AluOpType
EPS = 1e-5
P = 128
DM = 1024
HID = 2048
QKD = 128
GSZ = 256
KD = DM // P
NB = 32
SCALE = QKD ** 0.5
NEG = -1e30


def onehot256():
    def bucket_of(n):
        me = NB // 2
        if n < me:
            return n
        v = me + int(np.log(max(n, 1) / me) / np.log(128 / me) * (NB - me))
        return min(v, NB - 1)
    oh = np.zeros((NB, 511), np.float32)
    for m in range(511):
        oh[bucket_of(max(m - 255, 0)), m] = 256.0 * SCALE
    return oh


def build_program(NG=16, dbg=()):
    N = NG * GSZ
    nc = bass.Bass("TRN2", target_bir_lowering=False)
    dt = lambda n, s, k="ExternalInput": nc.dram_tensor(n, s, F32, kind=k)
    x_d = dt("x", [N, DM])
    out_d = dt("out", [N, DM], "ExternalOutput")
    lnw_d = dt("ln_w", [DM]); lnb_d = dt("ln_b", [DM])
    wh_d = dt("Wh", [DM, 2 * HID]); bh_d = dt("bh", [2 * HID])
    wqk_d = dt("Wqk", [DM, QKD]); bqk_d = dt("bqk", [QKD])
    qkw_d = dt("qk_w", [4, QKD]); qkb_d = dt("qk_b", [4, QKD])
    tbl_d = dt("rel_table", [NB])
    wout_d = dt("Wout", [HID, DM]); bout_d = dt("bout", [DM])
    wffn_d = dt("Wffn", [DM, DM])
    oh_d = dt("onehot", [NB, 511])
    f_hbm = dt("f_scr", [512], "Internal")
    woutbf_hbm = nc.dram_tensor("woutbf_scr", [HID, DM], BF16, kind="Internal")
    dbg_d = {nm: nc.dram_tensor("dbg_" + nm, shp, F32, kind="ExternalOutput")
             for nm, shp in dbg}

    T = Tracked(dma_streams={
        "xin": ("sync", 2), "wload": ("sync", 3),
        "wout": ("scalar", 4), "w2": ("scalar", 3),
        "xout": ("gpsimd", 2), "misc": ("gpsimd", 2),
    })
    st = ExitStack()
    sb = lambda n, s, d: st.enter_context(nc.sbuf_tensor(n, s, d))
    wh_bf = sb("wh_bf", [P, KD, 4096], BF16)
    wqk_bf = sb("wqk_bf", [P, KD, QKD], BF16)
    swt_bf = sb("swt_bf", [P, KD, DM], BF16)
    bhv_bf = sb("bhv_bf", [1, HID], BF16)
    ones_bf = sb("ones_bf", [1, P], BF16)
    bout_bf = sb("bout_bf", [1, DM], BF16)
    bmt = sb("bmt", [P, 2, GSZ], F32)
    S = sb("S", [P, HID], F32)
    S_bf = sb("S_bf", [P, HID], BF16)
    xb = [sb("xb0", [P, 2048], F32), sb("xb1", [P, 2048], F32)]
    yb = [sb("yb0", [P, 2048], F32), sb("yb1", [P, 2048], F32)]
    ao = sb("ao", [P, 2048], F32)
    t35 = sb("t35", [P, 1024], F32)
    t_bf = sb("t_bf", [P, 2048], BF16)
    xln_bf = sb("xln_bf", [P, 2048], BF16)
    xln_f32v = xln_bf.ap().bitcast(F32)
    tT = sb("tT", [P, KD, GSZ], BF16)
    xlnT = sb("xlnT", [P, KD, GSZ], BF16)
    v_bf = sb("v_bf", [P, 2, HID], BF16)
    gateT = sb("gateT", [P, 16, GSZ], BF16)
    qkT_sil = sb("qkT_sil", [P, GSZ], F32)
    ident = qkT_sil.ap()[:, 0:128]
    qqT = sb("qqT", [P, GSZ], BF16)
    qkkT = sb("qkkT", [P, GSZ], BF16)
    lqT = sb("lqT", [P, GSZ], BF16)
    lkT = sb("lkT", [P, GSZ], BF16)
    lk_nat = sb("lk_nat", [P, 2, QKD], BF16)
    attnT = sb("attnT", [P, 2, GSZ], BF16)
    u_at = sb("u_at", [P, GSZ], F32)
    zT = sb("zT", [P, 3, 512], BF16)
    wout_win = sb("wout_win", [P, 4, DM], BF16)
    lnw_bc = sb("lnw_bc", [P, DM], F32); lnb_bc = sb("lnb_bc", [P, DM], F32)
    bhg_c = sb("bhg_c", [P, 16], F32)
    bqk_c = sb("bqk_c", [P, 1], F32)
    qkw_c = sb("qkw_c", [P, 4], F32); qkb_c = sb("qkb_c", [P, 4], F32)
    w3s_c = sb("w3s_c", [P, 1], F32); b3s_c = sb("b3s_c", [P, 1], F32)
    ident_bf = sb("ident_bf", [P, P], BF16)
    neghalf = sb("neghalf", [P, 1], F32)
    epscol = sb("epscol", [P, 1], F32)
    gate_flat = gateT.ap().rearrange("p a b -> p (a b)")
    hank = gate_flat[:, 0:768].bitcast(F32)
    v_flat = v_bf.ap().rearrange("p a b -> p (a b)")
    oh_sb = v_flat[0:NB, 0:1022].bitcast(F32)
    masks = xb[1].ap()[:, 0:512].rearrange("p (a b) -> p a b", a=2)
    zt_flat = zT.ap().rearrange("p a b -> p (a b)")
    anti = zt_flat[:, 0:256].bitcast(F32)
    ww_flat = wout_win.ap().rearrange("p a b -> p (a b)")
    prm_row = ww_flat[0:16, 0:256].bitcast(F32)
    tbl_row = ww_flat[0:1, 256:320].bitcast(F32)
    tbl_col = ww_flat[0:NB, 320:322].bitcast(F32)
    f_sb = ww_flat[0:1, 1024:2046].bitcast(F32)
    stats = sb("stats", [P, 5 * 2 * 2 * 17], F32)
    _soff = {}
    def stat_cols(site, par, c):
        if site >= 3:
            par = 0
        key = (site, par, c)
        if key not in _soff:
            _soff[key] = len(_soff) * 17
        o = _soff[key]
        return dict(st12=stats[:, o:o + 12], ag2=stats[:, o + 12:o + 14],
                    ve=stats[:, o + 14:o + 15], rstd=stats[:, o + 15:o + 16],
                    bcol=stats[:, o + 16:o + 17],
                    tok=f"stat_{site}_{par}_{c}")
    ps = lambda n, s: st.enter_context(nc.psum_tensor(n, s, F32))
    pA = ps("pA", [P, 1024])
    pB = ps("pB", [P, 512])
    pC = ps("pC", [P, 1024])
    pD = ps("pD", [P, 1024])
    pE = ps("pE", [P, 512])

    def op(eng, fn, r=(), w=(), dma=None, tag=""):
        return T.op(eng, fn, reads=r, writes=w, dma=dma, tag=tag)

    def ln_site(site, g, src_ap_fn, src_toks, psum_src=False, per_chunk_toks=None,
                collect=None):
        emit = op if collect is None else (
            lambda *a, **k: collect.append(lambda a=a, k=k: op(*a, **k)))
        cols = []
        for c in range(2):
            if per_chunk_toks is not None:
                src_toks = [per_chunk_toks[c]]
            sc = stat_cols(site, g % 2, c)
            tok = sc["tok"]
            s0 = src_ap_fn(c, 0); s1 = src_ap_fn(c, 1)
            emit("vector", lambda e, sc=sc, s0=s0: e.bn_stats(sc["st12"][:, 0:6], s0),
               r=list(src_toks), w=[tok + "a"], tag=f"bn{site}{c}a_g{g}")
            emit("vector", lambda e, sc=sc, s1=s1: e.bn_stats(sc["st12"][:, 6:12], s1),
               r=list(src_toks), w=[tok + "b"], tag=f"bn{site}{c}b_g{g}")
            emit("vector", lambda e, sc=sc: e.bn_aggr(sc["ag2"], sc["st12"]),
               r=[tok + "a", tok + "b"], w=[tok + "g"], tag=f"ag{site}{c}_g{g}")
            emit("gpsimd", lambda e, sc=sc: e.tensor_tensor(out=sc["ve"], in0=sc["ag2"][:, 1:2],
                in1=epscol[:], op=ALU.add), r=[tok + "g", "epscol"], w=[tok + "v"], tag=f"ve{site}{c}_g{g}")
            emit("gpsimd", lambda e, sc=sc: e.tensor_tensor(out=sc["rstd"], in0=sc["ve"],
                in1=neghalf[:], op=ALU.pow), r=[tok + "v", "neghalf"], w=[tok + "r"], tag=f"pow{site}{c}_g{g}")
            emit("gpsimd", lambda e, sc=sc: e.tensor_scalar(out=sc["bcol"], in0=sc["ag2"][:, 0:1],
                scalar1=sc["rstd"], scalar2=-1.0, op0=ALU.mult, op1=ALU.mult),
               r=[tok + "g", tok + "r"], w=[tok + "bc"], tag=f"bc{site}{c}_g{g}")
            cols.append((sc["rstd"], sc["bcol"], [tok + "r", tok + "bc", tok + "g"]))
        return cols

    def emit_dbg(name, src, toks, shape=None):
        if name in dbg_d:
            op("gpsimd", lambda e: e.dma_start(out=dbg_d[name].ap(), in_=src),
               r=list(toks), dma="misc", tag="dbg_" + name)

    def bufs(g):
        par = g % 2
        return par, xb[par], yb[par], f"xb{par}", f"y{par}"

    def A_front_steps(g):
        par, xg, yg, xtok, ytok = bufs(g)
        steps = []
        emit = lambda *a, **k: steps.append(lambda a=a, k=k: op(*a, **k))
        c1 = ln_site(1, g, lambda c, h, xg=xg: xg[:, c * 1024 + h * 512: c * 1024 + (h + 1) * 512],
                     None, per_chunk_toks=[xtok + "c0", xtok + "c1"], collect=steps)
        for c in range(2):
            r_, b_, tks = c1[c]
            emit("scalar", lambda e, c=c, r_=r_, b_=b_, xg=xg, yg=yg: e.activation(
                out=yg[:, c * 1024:(c + 1) * 1024], in_=xg[:, c * 1024:(c + 1) * 1024],
                func=AF.Identity, bias=b_, scale=r_),
               r=[xtok + f"c{c}"] + tks, w=[ytok], tag=f"yact{c}_g{g}")
        c2 = ln_site(2, g, lambda c, h, yg=yg: yg[:, c * 1024 + h * 512: c * 1024 + (h + 1) * 512],
                     [ytok], collect=steps)
        for c in range(2):
            r_, b_, tks = c2[c]
            emit("scalar", lambda e, c=c, r_=r_, b_=b_, yg=yg: e.activation(
                out=t_bf[:, c * 1024:(c + 1) * 1024], in_=yg[:, c * 1024:(c + 1) * 1024],
                func=AF.Identity, bias=b_, scale=r_),
               r=[ytok] + tks, w=[f"t_bf{c}"], tag=f"tact{c}_g{g}")
            emit("vector", lambda e, c=c: e.tensor_tensor(
                out=t_bf[:, c * 1024:(c + 1) * 1024], in0=t_bf[:, c * 1024:(c + 1) * 1024],
                in1=lnw_bc[:], op=ALU.mult),
               r=[f"t_bf{c}", "lnw_bc"], w=[f"t_bf{c}"], tag=f"tw{c}_g{g}")
            emit("vector", lambda e, c=c: e.tensor_tensor(
                out=t_bf[:, c * 1024:(c + 1) * 1024], in0=t_bf[:, c * 1024:(c + 1) * 1024],
                in1=lnb_bc[:], op=ALU.add),
               r=[f"t_bf{c}", "lnb_bc"], w=[f"t_bf{c}"], tag=f"tb{c}_g{g}")
        return steps

    def A_front(g):
        for st_ in A_front_steps(g):
            st_()

    def A1a(g):
        for c in range(2):
            def trt(e, c=c):
                last = None
                for k in range(KD):
                    last = e.transpose(
                        pA[:, c * 512:(c + 1) * 512].bitcast(BF16)[:, k * P:(k + 1) * P],
                        t_bf[:, c * 1024 + k * P: c * 1024 + (k + 1) * P], ident_bf[:])
                return last
            op("tensor", trt, r=[f"t_bf{c}", "identbf", f"psA{c}"], w=[f"psA{c}"], tag=f"trt{c}_g{g}")
            def evt(e, c=c):
                return e.activation(out=tT[:, :, c * P:(c + 1) * P],
                    in_=pA[:, c * 512:(c + 1) * 512].bitcast(BF16).rearrange("p (k q) -> p k q", k=KD),
                    func=AF.Identity, scale=1.0)
            op("scalar", evt, r=[f"psA{c}"], w=[f"tT{c}"], tag=f"evt{c}_g{g}")
        def mmqk(e):
            last = None
            for k in range(KD):
                last = e.matmul(pB[:, 0:256], wqk_bf[:, k, :], tT[:, k, :],
                                start=(k == 0), stop=(k == KD - 1))
            return last
        op("tensor", mmqk, r=["wqk", "tT0", "tT1", "psB"], w=["psB"], tag=f"mmqk_g{g}")
        op("scalar", lambda e: e.activation(out=qkT_sil[:], in_=pB[:, 0:256], func=AF.Silu,
            bias=bqk_c[:], scale=1.0), r=["psB", "bqk_c"], w=["qkT_sil"], tag=f"silqk_g{g}")
        for nm, dst, wi, tok in (("qq", qqT, 0, "qqT"), ("qkk", qkkT, 2, "qkkT"),
                                 ("lq", lqT, 1, "lqT")):
            op("scalar", lambda e, dst=dst, wi=wi: e.activation(out=dst[:], in_=qkT_sil[:],
                func=AF.Identity, scale=qkw_c[:, wi:wi + 1], bias=qkb_c[:, wi:wi + 1]),
               r=["qkT_sil", "qkw_c", "qkb_c"], w=[tok], tag=f"{nm}_g{g}")
        op("scalar", lambda e: e.activation(out=lkT[:], in_=qkT_sil[:],
            func=AF.Identity, scale=w3s_c[:], bias=b3s_c[:]),
           r=["qkT_sil", "w3s_c", "b3s_c"], w=["lkT"], tag=f"lk_g{g}")
        def trlk(e):
            e.transpose(pB[:, 256:320].bitcast(BF16), lkT[:, 0:P], ident_bf[:])
            return e.transpose(pB[:, 320:384].bitcast(BF16), lkT[:, P:2 * P], ident_bf[:])
        op("tensor", trlk, r=["lkT", "identbf", "psB"], w=["psB"], tag=f"trlk_g{g}")
        for c in range(2):
            op("vector", lambda e, c=c: e.tensor_copy(lk_nat[:, c, :],
                pB[:, 256 + c * 64:320 + c * 64].bitcast(BF16)), r=["psB"], w=[f"lkn{c}"], tag=f"evlk{c}_g{g}")
        def mmsim(e):
            e.matmul(pE[:, 0:256], qkkT[:, 0:P], qqT[:], start=True, stop=True)
            return e.matmul(pE[:, 256:512], qkkT[:, P:2 * P], qqT[:], start=True, stop=True)
        op("tensor", mmsim, r=["qqT", "qkkT", "psE"], w=["psE"], tag=f"mmsim_g{g}")
        for jc in range(2):
            op("vector", lambda e, jc=jc: e.tensor_tensor(out=u_at[:],
                in0=pE[:, jc * 256:(jc + 1) * 256], in1=bmt[:, jc, :], op=ALU.add),
               r=["psE", "bmt", "uat"], w=["uat"], tag=f"bmtadd{jc}_g{g}")
            op("scalar", lambda e, jc=jc: e.activation(out=u_at[:], in_=u_at[:],
                func=AF.Relu, scale=1.0 / GSZ), r=["uat"], w=["uat"], tag=f"relu{jc}_g{g}")
            op("vector", lambda e, jc=jc: e.tensor_tensor(out=attnT[:, jc, :],
                in0=u_at[:], in1=u_at[:], op=ALU.mult),
               r=["uat"], w=[f"attnT{jc}", "uat"], tag=f"sq{jc}_g{g}")

    def A1b(g):
        for c in range(2):
            for n in range(4):
                half = n % 2
                pctok = f"psC{half}"
                def mmv(e, c=c, n=n, half=half):
                    e.matmul(pC[:, half * 512:(half + 1) * 512], ones_bf[:],
                             bhv_bf[0:1, n * 512:(n + 1) * 512], start=True, stop=False)
                    last = None
                    for k in range(KD):
                        last = e.matmul(pC[:, half * 512:(half + 1) * 512],
                                        tT[:, k, c * P:(c + 1) * P],
                                        wh_bf[:, k, n * 512:(n + 1) * 512],
                                        start=False, stop=(k == KD - 1))
                    return last
                op("tensor", mmv, r=[f"tT{c}", "ones_bf", "bhv_bf"] + [f"wh{k}" for k in range(KD)] + [pctok],
                   w=[pctok], tag=f"mmv{c}{n}_g{g}")
                op("scalar", lambda e, c=c, n=n, half=half: e.activation(
                    out=v_bf[:, c, n * 512:(n + 1) * 512], in_=pC[:, half * 512:(half + 1) * 512],
                    func=AF.Silu, scale=1.0), r=[pctok], w=[f"v_{c}_{n}"], tag=f"silv{c}{n}_g{g}")

    def A2(g):
        for eix in range(16):
            bank = eix % 2
            sub = (eix // 2) % 2
            off = bank * 512 + sub * 256
            pdtok = f"psD{bank}"
            def mmg(e, eix=eix, off=off):
                last = None
                for k in range(KD):
                    last = e.matmul(pD[:, off:off + 256],
                                    wh_bf[:, k, HID + eix * P: HID + (eix + 1) * P],
                                    tT[:, k, :], start=(k == 0), stop=(k == KD - 1))
                return last
            op("tensor", mmg, r=["tT0", "tT1"] + [f"wh{k}" for k in range(KD)] + [pdtok],
               w=[pdtok], tag=f"mmg{eix}_g{g}")
            op("scalar", lambda e, eix=eix, off=off: e.activation(
                out=gateT[:, eix, :], in_=pD[:, off:off + 256],
                func=AF.Silu, bias=bhg_c[:, eix:eix + 1], scale=1.0),
               r=[pdtok, "bhg_c"], w=[f"gateT{eix}"], tag=f"silg{eix}_g{g}")

    def B(g, mid=None):
        if g + 1 < NG:
            ld_x(g + 1)
        def emit_mmql(eix):
            pr = (eix // 2) % 3
            sub = (eix % 2) * 256
            pq, pbtok = ((pB, "psB"), (pE, "psE"), (pA, "psA0"))[pr]
            def mmql(e, eix=eix, pq=pq, sub=sub):
                e.matmul(pq[:, sub:sub + 256], v_bf[:, 0, eix * P:(eix + 1) * P],
                         attnT[:, 0, :], start=True, stop=False)
                e.matmul(pq[:, sub:sub + 256], v_bf[:, 1, eix * P:(eix + 1) * P],
                         attnT[:, 1, :], start=False, stop=False)
                return e.matmul(pq[:, sub:sub + 256], S_bf[:, eix * P:(eix + 1) * P],
                                lqT[:], start=False, stop=True)
            rd = [f"v_0_{eix // 4}", f"v_1_{eix // 4}", "attnT0", "attnT1", "S_bf", "lqT", pbtok]
            op("tensor", mmql, r=rd, w=[pbtok], tag=f"mmql{eix}_g{g}")
            if eix % 2 == 1:
                op("vector", lambda e, eix=eix, pq=pq, pr=pr: e.tensor_tensor(
                    out=zT[:, pr, :], in0=pq[:, 0:512],
                    in1=gateT.ap()[:, eix - 1:eix + 1, :].rearrange("p a b -> p (a b)"),
                    op=ALU.mult),
                   r=[pbtok, f"gateT{eix - 1}", f"gateT{eix}"], w=[f"zT{pr}"], tag=f"zt{eix}_g{g}")
        def emit_wld(eix):
            if g > 0:
                def ldwo(e, eix=eix):
                    return e.dma_start(out=wout_win[:, eix % 4, :],
                        in_=woutbf_hbm.ap()[eix * P:(eix + 1) * P, :])
                qe, qn = (("scalar", "wout") if eix % 2 == 0 else ("gpsimd", "misc"))
                op(qe, ldwo, r=[f"wobf{eix}"], w=[f"wow{eix % 4}"], dma=qn, tag=f"ldwow{eix}_g{g}")

        def emit_wout(eix):
            slot = (eix // 2) % 3
            zsub = (eix % 2) * 256
            for tc in range(2):
                pt = pC if tc == 0 else pD
                ptok = [f"psC{b}" for b in range(2)] if tc == 0 else [f"psD{b}" for b in range(2)]
                def mmwo(e, eix=eix, tc=tc, pt=pt, slot=slot, zsub=zsub):
                    e.matmul(pt[:, 0:512], zT[:, slot, zsub + tc * P:zsub + (tc + 1) * P],
                             wout_win[:, eix % 4, 0:512], start=(eix == 0), stop=(eix == 15))
                    last = e.matmul(pt[:, 512:1024], zT[:, slot, zsub + tc * P:zsub + (tc + 1) * P],
                                    wout_win[:, eix % 4, 512:1024], start=(eix == 0), stop=(eix == 15))
                    if eix == 0:
                        e.matmul(pt[:, 0:512], ones_bf[:], bout_bf[0:1, 0:512],
                                 start=False, stop=False)
                        last = e.matmul(pt[:, 512:1024], ones_bf[:], bout_bf[0:1, 512:1024],
                                        start=False, stop=False)
                    return last
                rd_extra = ["ones_bf", "bout_bf"] if eix == 0 else []
                op("tensor", mmwo, r=[f"zT{slot}", f"wow{eix % 4}"] + ptok + rd_extra,
                   w=ptok, tag=f"mmwo{eix}t{tc}_g{g}")
        def emit_build(r):
            def ldo(e, r=r):
                return e.dma_start(out=xln_f32v[:], in_=wout_d.ap()[r * P:(r + 1) * P, :])
            op("scalar", ldo, r=[], w=["xlnbf0", "xlnbf1"], dma="wout", tag=f"ldWo{r}")
            def cst(e, r=r):
                return e.tensor_copy(wout_win[:, r % 4, :], xln_f32v[:])
            op("vector", cst, r=["xlnbf0"], w=[f"wow{r % 4}"], tag=f"cWo{r}")
            def sto(e, r=r):
                return e.dma_start(out=woutbf_hbm.ap()[r * P:(r + 1) * P, :],
                                   in_=wout_win[:, r % 4, :])
            op("scalar", sto, r=[f"wow{r % 4}"], w=[f"wobf{r}"], dma="w2", tag=f"stWo{r}")
        steps = list(mid) if mid is not None else []
        si = 0
        if g == 0:
            emit_build(0)
            emit_build(1)
            emit_build(2)
        lag = 2
        for eix in range(4):
            emit_wld(eix)
        for eix in range(lag):
            emit_mmql(eix)
        for eix in range(lag, 16):
            take = 0 if eix < 4 else max(0, min(len(steps) - si, (len(steps) * (eix - 3)) // 12 - si))
            for _ in range(take):
                steps[si](); si += 1
            emit_mmql(eix)
            emit_wout(eix - lag)
            wln = eix - lag + 4
            if 4 <= wln < 16:
                emit_wld(wln)
            if g == 0 and eix + 1 < 16:
                emit_build(eix + 1)
        while si < len(steps):
            steps[si](); si += 1
        for eix in range(16 - lag, 16):
            emit_wout(eix)

    def C_pre(g):
        par, xg, yg, xtok, ytok = bufs(g)
        for n in range(4):
            dst = (pA[:, 0:512], "psA0") if n == 0 else \
                  (pA[:, 512:1024], "psA1") if n == 1 else \
                  (pE[:, 0:512], "psE") if n == 2 else (pB[:, 0:512], "psB")
            dap, dtok = dst
            def mmkv(e, n=n, dap=dap):
                e.matmul(dap, lk_nat[:, 0, :], v_bf[:, 0, n * 512:(n + 1) * 512], start=True, stop=False)
                return e.matmul(dap, lk_nat[:, 1, :], v_bf[:, 1, n * 512:(n + 1) * 512], start=False, stop=True)
            op("tensor", mmkv, r=["lkn0", "lkn1", f"v_0_{n}", f"v_1_{n}", dtok], w=[dtok], tag=f"mmkv{n}_g{g}")
            op("vector", lambda e, n=n, dap=dap: e.tensor_tensor(out=S[:, n * 512:(n + 1) * 512],
                in0=dap, in1=S[:, n * 512:(n + 1) * 512], op=ALU.add),
               r=[dtok, "S"], w=["S"], tag=f"Sup{n}_g{g}")


    def C_ao(g):
        par, xg, yg, xtok, ytok = bufs(g)
        for c in range(2):
            pt = pC if c == 0 else pD
            ptok = [f"psC{b}" for b in range(2)] if c == 0 else [f"psD{b}" for b in range(2)]
            op("vector", lambda e, c=c, pt=pt, yg=yg: e.tensor_tensor(out=ao[:, c * 1024:(c + 1) * 1024],
                in0=pt[:], in1=yg[:, c * 1024:(c + 1) * 1024], op=ALU.add),
               r=ptok + [ytok], w=[f"ao{c}"], tag=f"ao{c}_g{g}")
        emit_dbg("ao", ao[:, :], ["ao0", "ao1"])

    def S_cast(g):
        op("scalar", lambda e: e.activation(out=S_bf[:], in_=S[:], func=AF.Identity, scale=1.0),
           r=["S"], w=["S_bf"], tag=f"Sbf_g{g}")

    def C_mid1b(g):
        par, xg, yg, xtok, ytok = bufs(g)
        c3 = ln_site(3, g, lambda c, h: ao[:, c * 1024 + h * 512: c * 1024 + (h + 1) * 512],
                     None, per_chunk_toks=["ao0", "ao1"])
        c3m = [stat_cols(3, g % 2, c)["ag2"][:, 0:1] for c in range(2)]
        for c in range(2):
            r_, b_, tks = c3[c]
            m_ = c3m[c]
            op("vector", lambda e, c=c, r_=r_, m_=m_: e.tensor_scalar(
                out=t35[:], in0=ao[:, c * 1024:(c + 1) * 1024],
                scalar1=m_, scalar2=r_, op0=ALU.subtract, op1=ALU.mult),
               r=[f"ao{c}"] + tks, w=["t35x"], tag=f"t3a{c}_g{g}")
            op("vector", lambda e, c=c, xg=xg: e.tensor_tensor(out=xg[:, c * 1024:(c + 1) * 1024],
                in0=xg[:, c * 1024:(c + 1) * 1024], in1=t35[:], op=ALU.add),
               r=[xtok + f"c{c}", "t35x"], w=[xtok + f"c{c}"], tag=f"x2{c}_g{g}")
        emit_dbg("x2", xg[:, :], [xtok + "c0", xtok + "c1"])

    def C_mid2(g):
        par, xg, yg, xtok, ytok = bufs(g)
        c4 = ln_site(4, g, lambda c, h, xg=xg: xg[:, c * 1024 + h * 512: c * 1024 + (h + 1) * 512],
                     None, per_chunk_toks=[xtok + "c0", xtok + "c1"])
        c4m = [stat_cols(4, g % 2, c)["ag2"][:, 0:1] for c in range(2)]
        for c in range(2):
            r_, b_, tks = c4[c]
            m_ = c4m[c]
            op("vector", lambda e, c=c, r_=r_, m_=m_, xg=xg: e.tensor_scalar(
                out=xln_bf[:, c * 1024:(c + 1) * 1024], in0=xg[:, c * 1024:(c + 1) * 1024],
                scalar1=m_, scalar2=r_, op0=ALU.subtract, op1=ALU.mult),
               r=[xtok + f"c{c}"] + tks, w=[f"xlnbf{c}"], tag=f"xln{c}_g{g}")

    def C_ten(g):
        for c in range(2):
            def trx(e, c=c):
                last = None
                for k in range(KD):
                    last = e.transpose(
                        pA[:, c * 512:(c + 1) * 512].bitcast(BF16)[:, k * P:(k + 1) * P],
                        xln_bf[:, c * 1024 + k * P: c * 1024 + (k + 1) * P], ident_bf[:])
                return last
            op("tensor", trx, r=[f"xlnbf{c}", "identbf", f"psA{c}"], w=[f"psA{c}"], tag=f"trx{c}_g{g}")
            def evx(e, c=c):
                return e.activation(out=xlnT[:, :, c * P:(c + 1) * P],
                    in_=pA[:, c * 512:(c + 1) * 512].bitcast(BF16).rearrange("p (k q) -> p k q", k=KD),
                    func=AF.Identity, scale=1.0)
            op("scalar", evx, r=[f"psA{c}"], w=[f"xlnT{c}"], tag=f"evx{c}_g{g}")
        for c in range(2):
            pt = pC if c == 0 else pD
            ptok = [f"psC{b}" for b in range(2)] if c == 0 else [f"psD{b}" for b in range(2)]
            def mmf(e, c=c, pt=pt):
                last = None
                for k in range(KD):
                    e.matmul(pt[:, 0:512], xlnT[:, k, c * P:(c + 1) * P], swt_bf[:, k, 0:512],
                             start=(k == 0), stop=(k == KD - 1))
                    last = e.matmul(pt[:, 512:1024], xlnT[:, k, c * P:(c + 1) * P], swt_bf[:, k, 512:1024],
                                    start=(k == 0), stop=(k == KD - 1))
                return last
            op("tensor", mmf, r=[f"xlnT{c}"] + [f"swt{cc}" for cc in range(KD)] + ptok,
               w=ptok, tag=f"mmf{c}_g{g}")
            C_post_c(g, c)

    def C_post_c(g, c):
        par, xg, yg, xtok, ytok = bufs(g)
        if True:
            pt = pC if c == 0 else pD
            ptok = [f"psC{b}" for b in range(2)] if c == 0 else [f"psD{b}" for b in range(2)]
            sc = stat_cols(5, g % 2, c)
            tok = sc["tok"]
            op("vector", lambda e, sc=sc, pt=pt: e.bn_stats(sc["st12"][:, 0:6], pt[:, 0:512]),
               r=ptok, w=[tok + "a"], tag=f"bn5{c}a_g{g}")
            op("vector", lambda e, sc=sc, pt=pt: e.bn_stats(sc["st12"][:, 6:12], pt[:, 512:1024]),
               r=ptok, w=[tok + "b"], tag=f"bn5{c}b_g{g}")
            op("vector", lambda e, sc=sc: e.bn_aggr(sc["ag2"], sc["st12"]),
               r=[tok + "a", tok + "b"], w=[tok + "g"], tag=f"ag5{c}_g{g}")
            op("gpsimd", lambda e, sc=sc: e.tensor_tensor(out=sc["ve"], in0=sc["ag2"][:, 1:2],
                in1=epscol[:], op=ALU.add), r=[tok + "g", "epscol"], w=[tok + "v"], tag=f"ve5{c}_g{g}")
            op("gpsimd", lambda e, sc=sc: e.tensor_tensor(out=sc["rstd"], in0=sc["ve"],
                in1=neghalf[:], op=ALU.pow), r=[tok + "v", "neghalf"], w=[tok + "r"], tag=f"pow5{c}_g{g}")
            op("gpsimd", lambda e, sc=sc: e.tensor_scalar(out=sc["bcol"], in0=sc["ag2"][:, 0:1],
                scalar1=sc["rstd"], scalar2=-1.0, op0=ALU.mult, op1=ALU.mult),
               r=[tok + "g", tok + "r"], w=[tok + "bc"], tag=f"bc5{c}_g{g}")
            op("scalar", lambda e, c=c, sc=sc, pt=pt: e.activation(
                out=t35[:], in_=pt[:],
                func=AF.Identity, bias=sc["bcol"], scale=sc["rstd"]),
               r=ptok + [tok + "r", tok + "bc"], w=["t35x"], tag=f"t5a{c}_g{g}")
            op("vector", lambda e, c=c, xg=xg: e.tensor_tensor(out=xg[:, c * 1024:(c + 1) * 1024],
                in0=xg[:, c * 1024:(c + 1) * 1024], in1=t35[:], op=ALU.add),
               r=[xtok + f"c{c}", "t35x"], w=[xtok + f"c{c}"], tag=f"x3{c}_g{g}")

    def C_post(g):
        par, xg, yg, xtok, ytok = bufs(g)
        def stx(e, g=g, xg=xg):
            return e.dma_start(out=out_d.ap()[g * GSZ:(g + 1) * GSZ, :].rearrange("(c p) f -> p c f", p=P),
                               in_=xg.ap().rearrange("p (c f) -> p c f", c=2))
        op("gpsimd", stx, r=[xtok + "c0", xtok + "c1"], dma="xout", tag=f"stx_g{g}")


    op("gpsimd", lambda e: e.memset(neghalf[:], -0.5), w=["neghalf"], tag="ngh")
    op("gpsimd", lambda e: e.memset(epscol[:], EPS), w=["epscol"], tag="eps")
    op("gpsimd", lambda e: e.memset(ident[:], 0.0), w=["ident", "qkT_sil"], tag="id0")
    op("gpsimd", lambda e: e.affine_select(out=ident[:], in_=ident[:],
        compare_op=ALU.not_equal, fill=1.0, base=0, pattern=[[-1, P]],
        channel_multiplier=1), r=["ident"], w=["ident"], tag="id1")
    op("gpsimd", lambda e: e.memset(ident_bf[:], 0.0), w=["identbf"], tag="idb0")
    op("gpsimd", lambda e: e.affine_select(out=ident_bf[:], in_=ident_bf[:],
        compare_op=ALU.not_equal, fill=1.0, base=0, pattern=[[-1, P]],
        channel_multiplier=1), r=["identbf"], w=["identbf"], tag="idb1")
    op("gpsimd", lambda e: e.memset(anti[:], 0.0), w=["anti", "zT0"], tag="an0")
    op("gpsimd", lambda e: e.affine_select(out=anti[:], in_=anti[:],
        compare_op=ALU.not_equal, fill=1.0, base=-127, pattern=[[1, P]],
        channel_multiplier=1), r=["anti"], w=["anti"], tag="an1")
    for jc in range(2):
        op("gpsimd", lambda e, jc=jc: e.memset(masks[:, jc, :], 0.0),
           w=[f"mask{jc}", "xb1c0"], tag=f"mask0{jc}")
        op("gpsimd", lambda e, jc=jc: e.affine_select(out=masks[:, jc, :], in_=masks[:, jc, :],
            compare_op=ALU.is_ge, fill=NEG, base=-jc * P,
            pattern=[[1, GSZ]], channel_multiplier=-1),
           r=[f"mask{jc}"], w=[f"mask{jc}", "xb1c0"], tag=f"mask1{jc}")
    op("gpsimd", lambda e: e.memset(S[:], 0.0), w=["S"], tag="S0")
    op("gpsimd", lambda e: e.memset(S_bf[:], 0.0), w=["S_bf"], tag="Sbf0")

    def ld_x(g):
        def f(e, g=g):
            return e.dma_start(out=xb[g % 2].ap().rearrange("p (c f) -> p c f", c=2),
                in_=x_d.ap()[g * GSZ:(g + 1) * GSZ, :].rearrange("(c p) f -> p c f", p=P))
        op("sync", f, r=[], w=[f"xb{g % 2}c0", f"xb{g % 2}c1"], dma="xin", tag=f"ldx{g}")
    ld_x(0)

    def param_col(src_ap, nrows, dst_ap, dst_tok, nm):
        op("sync", lambda e: e.dma_start(out=prm_row[0:nrows, :], in_=src_ap),
           w=["prm", "wow0"], dma="wload", tag="ld_" + nm)
        op("tensor", lambda e: e.transpose(pB[:, 0:nrows], prm_row[0:nrows, :], ident[0:nrows, 0:nrows]),
           r=["prm", "ident", "psB"], w=["psB"], tag="tr_" + nm)
        op("vector", lambda e: e.tensor_copy(dst_ap, pB[:, 0:nrows]), r=["psB"], w=[dst_tok], tag="ev_" + nm)
    param_col(bh_d.ap()[HID:2 * HID].rearrange("(k p) -> k p", p=P), 16, bhg_c[:], "bhg_c", "bhg")
    param_col(bqk_d.ap().rearrange("(o p) -> o p", o=1), 1, bqk_c[:], "bqk_c", "bqk")
    param_col(qkw_d.ap(), 4, qkw_c[:], "qkw_c", "qkw")
    param_col(qkb_d.ap(), 4, qkb_c[:], "qkb_c", "qkb")
    op("vector", lambda e: e.tensor_scalar(out=w3s_c[:], in0=qkw_c[:, 3:4],
        scalar1=1.0 / GSZ, scalar2=None, op0=ALU.mult), r=["qkw_c"], w=["w3s_c"], tag="w3s")
    op("vector", lambda e: e.tensor_scalar(out=b3s_c[:], in0=qkb_c[:, 3:4],
        scalar1=1.0 / GSZ, scalar2=None, op0=ALU.mult), r=["qkb_c"], w=["b3s_c"], tag="b3s")

    op("sync", lambda e: e.dma_start(out=lnw_bc[:], in_=AP(lnw_d, 0, [[0, P], [1, DM]])),
       w=["lnw_bc"], dma="wload", tag="ld_lnwbc")
    op("sync", lambda e: e.dma_start(out=lnb_bc[:], in_=AP(lnb_d, 0, [[0, P], [1, DM]])),
       w=["lnb_bc"], dma="wload", tag="ld_lnbbc")
    op("gpsimd", lambda e: e.dma_start(out=bhv_bf[:], in_=bh_d.ap()[0:HID].rearrange("(o n) -> o n", o=1)),
       w=["bhv_bf"], dma="misc", tag="ld_bhv")
    op("gpsimd", lambda e: e.dma_start(out=bout_bf[:], in_=bout_d.ap().rearrange("(o n) -> o n", o=1)),
       w=["bout_bf"], dma="misc", tag="ld_bout")
    op("gpsimd", lambda e: e.memset(ones_bf[:], 1.0), w=["ones_bf"], tag="ones")

    op("sync", lambda e: e.dma_start(out=tbl_row[:], in_=tbl_d.ap().rearrange("(o n) -> o n", o=1)),
       w=["tbl_row", "wow0"], dma="wload", tag="ld_tbl")
    op("sync", lambda e: e.dma_start(out=oh_sb[:], in_=oh_d.ap()), w=["oh_sb", "v_0_0"], dma="wload", tag="ld_oh")
    op("tensor", lambda e: e.transpose(pB[0:NB, 28:29], tbl_row[0:1, :], ident[0:1, 0:1]),
       r=["tbl_row", "ident", "psB"], w=["psB"], tag="tr_tbl")
    op("vector", lambda e: e.tensor_copy(tbl_col[:], pB[0:NB, 28:29]), r=["psB"], w=["tbl_col"], tag="ev_tbl")
    op("tensor", lambda e: e.matmul(pE[0:1, 0:511], tbl_col[:], oh_sb[:], start=True, stop=True),
       r=["tbl_col", "oh_sb"], w=["psE"], tag="mm_F")
    op("vector", lambda e: e.tensor_copy(f_sb[:], pE[0:1, 0:511]), r=["psE"], w=["f_sb", "wow1"], tag="ev_F")
    op("gpsimd", lambda e: e.dma_start(out=f_hbm.ap()[0:511].rearrange("(o n) -> o n", o=1), in_=f_sb[:]),
       r=["f_sb"], w=["f_hbm"], dma="misc", tag="st_F")
    op("gpsimd", lambda e: e.dma_start(out=hank[:], in_=AP(f_hbm, 0, [[1, P], [1, 384]])),
       r=["f_hbm"], w=["hank", "gateT0", "gateT1", "gateT2"], dma="misc", tag="ld_hank")
    def mm_bias(e):
        e.matmul(pE[:, 0:256], anti[:], hank[:, 128:384], start=True, stop=True)
        return e.matmul(pE[:, 256:512], anti[:], hank[:, 0:256], start=True, stop=True)
    op("tensor", mm_bias, r=["anti", "hank", "psE"], w=["psE"], tag="mm_bias")
    for jc in range(2):
        op("vector", lambda e, jc=jc: e.tensor_tensor(out=bmt[:, jc, :], in0=pE[:, jc * 256:(jc + 1) * 256],
            in1=masks[:, jc, :], op=ALU.add), r=["psE", f"mask{jc}"], w=["bmt"], tag=f"bmt{jc}")

    op("sync", lambda e: e.dma_start(out=ao.ap()[:, 0:1024].rearrange("p (k q) -> p k q", k=KD),
        in_=wqk_d.ap().rearrange("(k p) q -> p k q", p=P)), w=["ao0", "ao1"], dma="wload", tag="ldWqk")
    op("scalar", lambda e: e.activation(out=wqk_bf.ap().rearrange("p k q -> p (k q)"),
        in_=ao[:, 0:1024], func=AF.Identity, scale=1.0), r=["ao0"], w=["wqk"], tag="cWqk")
    A_front(0)
    v_stage = v_bf.ap().rearrange("p a b -> p (a b)").bitcast(F32)
    gate_stage = gateT.ap().rearrange("p a b -> p (a b)").bitcast(F32)
    wh_chan = [("wload", "sync", ao, ["ao0", "ao1"]),
               ("w2", "scalar", gate_stage, ["gateT%d" % i for i in range(16)]),
               ("xin", "sync", v_stage, ["v_%d_%d" % (c, n) for c in range(2) for n in range(4)])]
    for k in range(KD):
        if True:
            q, qeng, stg, stok = wh_chan[k % 3]
            for h in range(2):
                def ldw(e, k=k, h=h, stg=stg):
                    return e.dma_start(out=stg[:],
                        in_=wh_d.ap().rearrange("(k p) m -> p k m", p=P)[:, k, h * 2048:(h + 1) * 2048])
                op(qeng, ldw, r=[], w=stok, dma=q, tag=f"ldWh{k}_{h}")
                def cst(e, k=k, h=h, stg=stg):
                    return e.tensor_copy(wh_bf[:, k, h * 2048:(h + 1) * 2048], stg[:])
                op("vector", cst, r=[stok[0]], w=[f"wh{k}"], tag=f"cWh{k}_{h}")
    A1a(0)
    A1b(0)
    A2(0)
    for c in range(KD):
        def ldf(e, c=c):
            return e.dma_start(out=t35[:],
                in_=wffn_d.ap()[c * P:(c + 1) * P, :])
        op("sync", ldf, r=[], w=["t35x"], dma="wload", tag=f"ldWf{c}")
        def sgn(e, c=c):
            return e.activation(out=xln_bf[:, 0:1024], in_=t35[:], func=AF.Sign, scale=1.0)
        op("scalar", sgn, r=["t35x"], w=["xlnbf0", "xlnbf1"], tag=f"sgnWf{c}")
        def trf(e, c=c):
            last = None
            for r in range(KD):
                last = e.transpose(pA[:, 0:1024].bitcast(BF16)[:, r * P:(r + 1) * P],
                                   xln_bf[:, r * P:(r + 1) * P], ident_bf[:])
            return last
        op("tensor", trf, r=["xlnbf0", "identbf", "psA0"], w=["psA0"], tag=f"trWf{c}")
        def evf(e, c=c):
            return e.tensor_copy(swt_bf[:, :, c * P:(c + 1) * P],
                                 pA[:, 0:512].bitcast(BF16).rearrange("p (r q) -> p r q", r=KD))
        op("vector", evf, r=["psA0"], w=[f"swt{c}"], tag=f"evWf{c}")
    for g in range(NG):
        B(g, mid=A_front_steps(g + 1) if g + 1 < NG else None)
        C_pre(g)
        C_ao(g)
        if g + 1 < NG:
            A1a(g + 1)
            A1b(g + 1)
            A2(g + 1)
        S_cast(g)
        C_mid1b(g)
        C_mid2(g)
        C_ten(g)
        C_post(g)

    with nc.Block() as block:
        semctx = ExitStack()
        sems = {n: semctx.enter_context(nc.semaphore(n)) for n in T.sem_names()}
        T.prepare(verbose=True)
        @block.tensor
        def _(t): T.emit_engine("tensor", t, sems)
        @block.vector
        def _(v): T.emit_engine("vector", v, sems)
        @block.scalar
        def _(s): T.emit_engine("scalar", s, sems)
        @block.gpsimd
        def _(gp): T.emit_engine("gpsimd", gp, sems)
        @block.sync
        def _(sy): T.emit_engine("sync", sy, sems, final_waiter=True)
        semctx.close()
    st.close()
    return nc


def make_inputs(inp, core, NG=16):
    b = core
    m = {
        "x": np.ascontiguousarray(np.asarray(inp["L0_attn_x"][b], np.float32)[:NG * GSZ]),
        "ln_w": np.asarray(inp["L0_attn_ln_w"], np.float32),
        "ln_b": np.asarray(inp["L0_attn_ln_b"], np.float32),
        "Wh": np.asarray(inp["L0_attn_Wh"], np.float32),
        "bh": np.asarray(inp["L0_attn_bh"], np.float32),
        "Wqk": np.asarray(inp["L0_attn_Wqk"], np.float32),
        "bqk": np.asarray(inp["L0_attn_bqk"], np.float32),
        "qk_w": np.asarray(inp["L0_attn_qk_w"], np.float32),
        "qk_b": np.asarray(inp["L0_attn_qk_b"], np.float32),
        "rel_table": np.asarray(inp["L0_attn_rel_table"], np.float32),
        "Wout": np.asarray(inp["L0_attn_Wout"], np.float32),
        "bout": np.asarray(inp["L0_attn_bout"], np.float32),
        "Wffn": np.asarray(inp["L0_ffn_weight"], np.float32),
        "onehot": onehot256(),
    }
    return m


_NC = 8

def kernel(**inputs):
    nc = build_program(NG=16)
    in_maps = [make_inputs(inputs, core) for core in range(_NC)]
    res = run_bass_kernel_spmd(nc, in_maps, core_ids=list(range(_NC)))
    return np.stack([res.results[c]["out"] for c in range(_NC)], axis=0)


# version 07 (final)
# speedup vs baseline: 1.0877x; 1.0877x over previous
import numpy as np
from dataclasses import dataclass, field
from contextlib import ExitStack
import concourse.bass as bass
import concourse.mybir as mybir
from bass_rust import AP
from concourse.bass_utils import run_bass_kernel_spmd


from dataclasses import dataclass, field

@dataclass
class Item:
    idx: int
    eng: str
    fn: object
    reads: tuple
    writes: tuple
    stream: str
    slot: int
    seq: int
    tag: str
    deps: set = field(default_factory=set)
    rank: int = 0

class Tracked:
    def __init__(self, dma_streams=None):
        self.items = []
        self.last_w = {}
        self.readers = {}
        self.streams = dict(dma_streams or {})
        self.stream_items = {s: [] for s in self.streams}

    def sem_names(self):
        names = ["tensor", "vector", "scalar", "gpsimd"]
        for s, (eng, depth) in self.streams.items():
            names += [f"dma_{s}_{j}" for j in range(depth)]
        return names

    def op(self, eng, fn, reads=(), writes=(), dma=None, tag=""):
        idx = len(self.items)
        deps = set()
        for t in reads:
            if t in self.last_w:
                deps.add(self.last_w[t])
        for t in writes:
            if t in self.last_w:
                deps.add(self.last_w[t])
            deps |= self.readers.get(t, set())
        deps.discard(idx)
        slot = seq = 0
        if dma:
            qeng, depth = self.streams[dma]
            assert eng == qeng, f"stream {dma} belongs to queue {qeng}"
            lst = self.stream_items[dma]
            seq = len(lst)
            slot = seq % depth
            if seq >= depth:
                deps.add(lst[seq - depth])
            lst.append(idx)
        it = Item(idx, eng, fn, tuple(reads), tuple(writes),
                  dma or '', slot, seq, tag, deps)
        self.items.append(it)
        for t in reads:
            self.readers.setdefault(t, set()).add(idx)
        for t in writes:
            self.last_w[t] = idx
            self.readers[t] = set()
        return idx

    def _semkey(self, it):
        return f"dma_{it.stream}_{it.slot}" if it.stream else it.eng

    def _assign_ranks(self, needed):
        self.counts = {}
        for it in self.items:
            key = self._semkey(it)
            it.rank = 0
            if it.stream:
                depth = self.streams[it.stream][1]
                it.rank = 16 * (it.seq // depth + 1)
                self.counts[key] = max(self.counts.get(key, 0), it.rank)
            elif it.idx in needed:
                self.counts[key] = self.counts.get(key, 0) + 1
                it.rank = self.counts[key]

    def _plan_waits(self):
        clock_of = {}
        rc = {}
        by_rank = {}
        waits = {}
        used = set()

        def merge(dst, src):
            for k, v in src.items():
                if dst.get(k, 0) < v:
                    dst[k] = v

        for it in self.items:
            eng_rc = rc.setdefault(it.eng, {})
            tgt = {}
            for di in it.deps:
                dit = self.items[di]
                if dit.rank <= 0:
                    continue
                key = self._semkey(dit)
                if tgt.get(key, (0, None))[0] < dit.rank:
                    tgt[key] = (dit.rank, di)
            w = []
            for key, (v, di) in sorted(tgt.items()):
                if eng_rc.get(key, 0) >= v:
                    continue
                w.append((key, v))
                used.add(di)
                merge(eng_rc, clock_of.get(di, {}))
                if eng_rc.get(key, 0) < v:
                    eng_rc[key] = v
            waits[it.idx] = w
            if it.rank:
                key = self._semkey(it)
                c = dict(eng_rc)
                c[key] = max(c.get(key, 0), it.rank)
                clock_of[it.idx] = c
        return waits, used

    def prepare(self, skip_same_engine=("tensor",), verbose=False):
        for it in self.items:
            it.deps = {d for d in it.deps
                       if not (self.items[d].eng == it.eng
                               and not self.items[d].stream and not it.stream
                               and it.eng in skip_same_engine)}
        needed = {d for it in self.items for d in it.deps}
        self._assign_ranks(needed)
        w1, used = self._plan_waits()
        needed2 = {i for i in used if not self.items[i].stream}
        needed2 |= {i for i in needed if self.items[i].stream}
        self._assign_ranks(needed2)
        self.waits, used2 = self._plan_waits()
        self.nwait = sum(len(v) for v in self.waits.values())
        if verbose:
            ninc = sum(1 for it in self.items if it.rank and not it.stream)
            print(f"[trk] items={len(self.items)} incs={ninc} waits={self.nwait} "
                  f"dmas={sum(1 for it in self.items if it.stream)}")

    def emit_engine(self, name, eng_obj, sems, final_waiter=False):
        seen = {}
        for it in self.items:
            if it.eng != name:
                continue
            for key, v in self.waits.get(it.idx, ()):
                eng_obj.wait_ge(sems[key], v)
                seen[key] = max(seen.get(key, 0), v)
            inst = it.fn(eng_obj)
            if it.rank:
                assert inst is not None, f"needs inc but fn returned None: {it.tag}"
                inst.then_inc(sems[self._semkey(it)], 16 if it.stream else 1)
        if final_waiter:
            for key, v in sorted(self.counts.items()):
                if key.startswith('dma_') and seen.get(key, 0) < v:
                    eng_obj.wait_ge(sems[key], v)



"""Megakernel builder for mk_sandwich_d1 (linear_attn sandwich block, depth 1).

Per core: one batch element, seq = NG*256 tokens, d_model 1024.
Token-major residual stream; dm-major (transposed) activations feed TensorE.

Layout conventions (tok = c*128 + p within a group of 256):
  x/y/ao/t35 buffers: (128, 2048) f32 viewed per tok-chunk c as [:, c*1024:(c+1)*1024]
  t_bf/xln_bf:        (128, 2048) bf16 same
  tT/xlnT:            (128, 8, 256) bf16  [dm-chunk k on dim1, tok on dim2]
  v_bf:               (128, 2, 2048) bf16 [tok-chunk c on dim1, hid e on dim2]
  gateT:              (128, 16, 256) bf16 [e-chunk on dim1, tok on dim2]
  Wh_bf:              (128, 8, 4096) bf16 [dm-chunk k, hid-out h]
  Wqk_bf:             (128, 8, 128)  bf16
  sWT_bf:             (128, 8, 1024) bf16 [i-chunk r, out o]
  S:                  (128, 2048) f32 (qk-dim d on partitions, hid e free)
PSUM banks: pA(0-1), pB(2), pC(3-4), pD(5-6), pE(7)
"""

F32 = mybir.dt.float32
BF16 = mybir.dt.bfloat16
AF = mybir.ActivationFunctionType
ALU = mybir.AluOpType
EPS = 1e-5
P = 128
DM = 1024
HID = 2048
QKD = 128
GSZ = 256
KD = DM // P
NB = 32
SCALE = QKD ** 0.5
NEG = -1e30


def onehot256():
    def bucket_of(n):
        me = NB // 2
        if n < me:
            return n
        v = me + int(np.log(max(n, 1) / me) / np.log(128 / me) * (NB - me))
        return min(v, NB - 1)
    oh = np.zeros((NB, 511), np.float32)
    for m in range(511):
        oh[bucket_of(max(m - 255, 0)), m] = 256.0 * SCALE
    return oh


def build_program(NG=16, dbg=()):
    N = NG * GSZ
    nc = bass.Bass("TRN2", target_bir_lowering=False)
    dt = lambda n, s, k="ExternalInput": nc.dram_tensor(n, s, F32, kind=k)
    x_d = dt("x", [N, DM])
    out_d = dt("out", [N, DM], "ExternalOutput")
    lnw_d = dt("ln_w", [DM]); lnb_d = dt("ln_b", [DM])
    wh_d = dt("Wh", [DM, 2 * HID]); bh_d = dt("bh", [2 * HID])
    wqk_d = dt("Wqk", [DM, QKD]); bqk_d = dt("bqk", [QKD])
    qkw_d = dt("qk_w", [4, QKD]); qkb_d = dt("qk_b", [4, QKD])
    tbl_d = dt("rel_table", [NB])
    wout_d = dt("Wout", [HID, DM]); bout_d = dt("bout", [DM])
    wffn_d = dt("Wffn", [DM, DM])
    oh_d = dt("onehot", [NB, 511])
    f_hbm = dt("f_scr", [512], "Internal")
    woutbf_hbm = nc.dram_tensor("woutbf_scr", [HID, DM], BF16, kind="Internal")
    dbg_d = {nm: nc.dram_tensor("dbg_" + nm, shp, F32, kind="ExternalOutput")
             for nm, shp in dbg}

    T = Tracked(dma_streams={
        "xin": ("sync", 4), "wload": ("sync", 5),
        "wout": ("scalar", 4), "w2": ("scalar", 5),
        "xout": ("gpsimd", 2), "misc": ("gpsimd", 2),
    })
    st = ExitStack()
    sb = lambda n, s, d: st.enter_context(nc.sbuf_tensor(n, s, d))
    wh_bf = sb("wh_bf", [P, KD, 4096], BF16)
    wqk_bf = sb("wqk_bf", [P, KD, QKD], BF16)
    swt_bf = sb("swt_bf", [P, KD, DM], BF16)
    bhv_bf = sb("bhv_bf", [1, HID], BF16)
    ones_bf = sb("ones_bf", [1, P], BF16)
    bout_bf = sb("bout_bf", [1, DM], BF16)
    bmt = sb("bmt", [P, 2, GSZ], F32)
    S = sb("S", [P, HID], F32)
    S_bf = sb("S_bf", [P, HID], BF16)
    xb = [sb("xb0", [P, 2048], F32), sb("xb1", [P, 2048], F32)]
    yb = [sb("yb0", [P, 2048], F32), sb("yb1", [P, 2048], F32)]
    ao = sb("ao", [P, 2048], F32)
    t35 = sb("t35", [P, 1024], F32)
    t_bf = sb("t_bf", [P, 2048], BF16)
    xln_bf = sb("xln_bf", [P, 2048], BF16)
    xln_f32v = xln_bf.ap().bitcast(F32)
    tT = sb("tT", [P, KD, GSZ], BF16)
    xlnT = sb("xlnT", [P, KD, GSZ], BF16)
    v_bf = sb("v_bf", [P, 2, HID], BF16)
    gateT = sb("gateT", [P, 16, GSZ], BF16)
    qkT_sil = sb("qkT_sil", [P, GSZ], F32)
    ident = qkT_sil.ap()[:, 0:128]
    qqT = sb("qqT", [P, GSZ], BF16)
    qkkT = sb("qkkT", [P, GSZ], BF16)
    lqT = sb("lqT", [P, GSZ], BF16)
    lkT = sb("lkT", [P, GSZ], BF16)
    lk_nat = sb("lk_nat", [P, 2, QKD], BF16)
    attnT = sb("attnT", [P, 2, GSZ], BF16)
    u_at = sb("u_at", [P, GSZ], F32)
    zT = sb("zT", [P, 3, 512], BF16)
    wout_win = sb("wout_win", [P, 4, DM], BF16)
    lnw_bc = sb("lnw_bc", [P, DM], F32); lnb_bc = sb("lnb_bc", [P, DM], F32)
    bhg_c = sb("bhg_c", [P, 16], F32)
    bqk_c = sb("bqk_c", [P, 1], F32)
    qkw_c = sb("qkw_c", [P, 4], F32); qkb_c = sb("qkb_c", [P, 4], F32)
    w3s_c = sb("w3s_c", [P, 1], F32); b3s_c = sb("b3s_c", [P, 1], F32)
    ident_bf = sb("ident_bf", [P, P], BF16)
    neghalf = sb("neghalf", [P, 1], F32)
    epscol = sb("epscol", [P, 1], F32)
    gate_flat = gateT.ap().rearrange("p a b -> p (a b)")
    hank = gate_flat[:, 0:768].bitcast(F32)
    v_flat = v_bf.ap().rearrange("p a b -> p (a b)")
    oh_sb = v_flat[0:NB, 0:1022].bitcast(F32)
    masks = xb[1].ap()[:, 0:512].rearrange("p (a b) -> p a b", a=2)
    zt_flat = zT.ap().rearrange("p a b -> p (a b)")
    anti = zt_flat[:, 0:256].bitcast(F32)
    ww_flat = wout_win.ap().rearrange("p a b -> p (a b)")
    prm_row = ww_flat[0:16, 0:256].bitcast(F32)
    tbl_row = ww_flat[0:1, 256:320].bitcast(F32)
    tbl_col = ww_flat[0:NB, 320:322].bitcast(F32)
    f_sb = ww_flat[0:1, 1024:2046].bitcast(F32)
    stats = sb("stats", [P, 5 * 2 * 2 * 17], F32)
    _soff = {}
    def stat_cols(site, par, c):
        if site >= 3:
            par = 0
        key = (site, par, c)
        if key not in _soff:
            _soff[key] = len(_soff) * 17
        o = _soff[key]
        return dict(st12=stats[:, o:o + 12], ag2=stats[:, o + 12:o + 14],
                    ve=stats[:, o + 14:o + 15], rstd=stats[:, o + 15:o + 16],
                    bcol=stats[:, o + 16:o + 17],
                    tok=f"stat_{site}_{par}_{c}")
    ps = lambda n, s: st.enter_context(nc.psum_tensor(n, s, F32))
    pA = ps("pA", [P, 1024])
    pB = ps("pB", [P, 512])
    pC = ps("pC", [P, 1024])
    pD = ps("pD", [P, 1024])
    pE = ps("pE", [P, 512])

    def op(eng, fn, r=(), w=(), dma=None, tag=""):
        return T.op(eng, fn, reads=r, writes=w, dma=dma, tag=tag)

    def ln_site(site, g, src_ap_fn, src_toks, psum_src=False, per_chunk_toks=None,
                collect=None):
        emit = op if collect is None else (
            lambda *a, **k: collect.append(lambda a=a, k=k: op(*a, **k)))
        cols = []
        for c in range(2):
            if per_chunk_toks is not None:
                src_toks = [per_chunk_toks[c]]
            sc = stat_cols(site, g % 2, c)
            tok = sc["tok"]
            s0 = src_ap_fn(c, 0); s1 = src_ap_fn(c, 1)
            emit("vector", lambda e, sc=sc, s0=s0: e.bn_stats(sc["st12"][:, 0:6], s0),
               r=list(src_toks), w=[tok + "a"], tag=f"bn{site}{c}a_g{g}")
            emit("vector", lambda e, sc=sc, s1=s1: e.bn_stats(sc["st12"][:, 6:12], s1),
               r=list(src_toks), w=[tok + "b"], tag=f"bn{site}{c}b_g{g}")
            emit("vector", lambda e, sc=sc: e.bn_aggr(sc["ag2"], sc["st12"]),
               r=[tok + "a", tok + "b"], w=[tok + "g"], tag=f"ag{site}{c}_g{g}")
            emit("gpsimd", lambda e, sc=sc: e.tensor_tensor(out=sc["ve"], in0=sc["ag2"][:, 1:2],
                in1=epscol[:], op=ALU.add), r=[tok + "g", "epscol"], w=[tok + "v"], tag=f"ve{site}{c}_g{g}")
            emit("gpsimd", lambda e, sc=sc: e.tensor_tensor(out=sc["rstd"], in0=sc["ve"],
                in1=neghalf[:], op=ALU.pow), r=[tok + "v", "neghalf"], w=[tok + "r"], tag=f"pow{site}{c}_g{g}")
            emit("gpsimd", lambda e, sc=sc: e.tensor_scalar(out=sc["bcol"], in0=sc["ag2"][:, 0:1],
                scalar1=sc["rstd"], scalar2=-1.0, op0=ALU.mult, op1=ALU.mult),
               r=[tok + "g", tok + "r"], w=[tok + "bc"], tag=f"bc{site}{c}_g{g}")
            cols.append((sc["rstd"], sc["bcol"], [tok + "r", tok + "bc", tok + "g"]))
        return cols

    def emit_dbg(name, src, toks, shape=None):
        if name in dbg_d:
            op("gpsimd", lambda e: e.dma_start(out=dbg_d[name].ap(), in_=src),
               r=list(toks), dma="misc", tag="dbg_" + name)

    def bufs(g):
        par = g % 2
        return par, xb[par], yb[par], f"xb{par}", f"y{par}"

    def A_front_steps(g):
        par, xg, yg, xtok, ytok = bufs(g)
        steps = []
        emit = lambda *a, **k: steps.append(lambda a=a, k=k: op(*a, **k))
        c1 = ln_site(1, g, lambda c, h, xg=xg: xg[:, c * 1024 + h * 512: c * 1024 + (h + 1) * 512],
                     None, per_chunk_toks=[xtok + "c0", xtok + "c1"], collect=steps)
        for c in range(2):
            r_, b_, tks = c1[c]
            emit("scalar", lambda e, c=c, r_=r_, b_=b_, xg=xg, yg=yg: e.activation(
                out=yg[:, c * 1024:(c + 1) * 1024], in_=xg[:, c * 1024:(c + 1) * 1024],
                func=AF.Identity, bias=b_, scale=r_),
               r=[xtok + f"c{c}"] + tks, w=[ytok], tag=f"yact{c}_g{g}")
        c2 = ln_site(2, g, lambda c, h, yg=yg: yg[:, c * 1024 + h * 512: c * 1024 + (h + 1) * 512],
                     [ytok], collect=steps)
        for c in range(2):
            r_, b_, tks = c2[c]
            emit("scalar", lambda e, c=c, r_=r_, b_=b_, yg=yg: e.activation(
                out=t_bf[:, c * 1024:(c + 1) * 1024], in_=yg[:, c * 1024:(c + 1) * 1024],
                func=AF.Identity, bias=b_, scale=r_),
               r=[ytok] + tks, w=[f"t_bf{c}"], tag=f"tact{c}_g{g}")
            emit("vector", lambda e, c=c: e.tensor_tensor(
                out=t_bf[:, c * 1024:(c + 1) * 1024], in0=t_bf[:, c * 1024:(c + 1) * 1024],
                in1=lnw_bc[:], op=ALU.mult),
               r=[f"t_bf{c}", "lnw_bc"], w=[f"t_bf{c}"], tag=f"tw{c}_g{g}")
            emit("vector", lambda e, c=c: e.tensor_tensor(
                out=t_bf[:, c * 1024:(c + 1) * 1024], in0=t_bf[:, c * 1024:(c + 1) * 1024],
                in1=lnb_bc[:], op=ALU.add),
               r=[f"t_bf{c}", "lnb_bc"], w=[f"t_bf{c}"], tag=f"tb{c}_g{g}")
        return steps

    def A_front(g):
        for st_ in A_front_steps(g):
            st_()

    def A1a(g):
        for c in range(2):
            def trt(e, c=c):
                last = None
                for k in range(KD):
                    last = e.transpose(
                        pA[:, c * 512:(c + 1) * 512].bitcast(BF16)[:, k * P:(k + 1) * P],
                        t_bf[:, c * 1024 + k * P: c * 1024 + (k + 1) * P], ident_bf[:])
                return last
            op("tensor", trt, r=[f"t_bf{c}", "identbf", f"psA{c}"], w=[f"psA{c}"], tag=f"trt{c}_g{g}")
            def evt(e, c=c):
                return e.activation(out=tT[:, :, c * P:(c + 1) * P],
                    in_=pA[:, c * 512:(c + 1) * 512].bitcast(BF16).rearrange("p (k q) -> p k q", k=KD),
                    func=AF.Identity, scale=1.0)
            op("scalar", evt, r=[f"psA{c}"], w=[f"tT{c}"], tag=f"evt{c}_g{g}")
        def mmqk(e):
            last = None
            for k in range(KD):
                last = e.matmul(pB[:, 0:256], wqk_bf[:, k, :], tT[:, k, :],
                                start=(k == 0), stop=(k == KD - 1))
            return last
        op("tensor", mmqk, r=["wqk", "tT0", "tT1", "psB"], w=["psB"], tag=f"mmqk_g{g}")
        op("scalar", lambda e: e.activation(out=qkT_sil[:], in_=pB[:, 0:256], func=AF.Silu,
            bias=bqk_c[:], scale=1.0), r=["psB", "bqk_c"], w=["qkT_sil"], tag=f"silqk_g{g}")
        for nm, dst, wi, tok in (("qq", qqT, 0, "qqT"), ("qkk", qkkT, 2, "qkkT"),
                                 ("lq", lqT, 1, "lqT")):
            op("scalar", lambda e, dst=dst, wi=wi: e.activation(out=dst[:], in_=qkT_sil[:],
                func=AF.Identity, scale=qkw_c[:, wi:wi + 1], bias=qkb_c[:, wi:wi + 1]),
               r=["qkT_sil", "qkw_c", "qkb_c"], w=[tok], tag=f"{nm}_g{g}")
        op("scalar", lambda e: e.activation(out=lkT[:], in_=qkT_sil[:],
            func=AF.Identity, scale=w3s_c[:], bias=b3s_c[:]),
           r=["qkT_sil", "w3s_c", "b3s_c"], w=["lkT"], tag=f"lk_g{g}")
        def trlk(e):
            e.transpose(pB[:, 256:320].bitcast(BF16), lkT[:, 0:P], ident_bf[:])
            return e.transpose(pB[:, 320:384].bitcast(BF16), lkT[:, P:2 * P], ident_bf[:])
        op("tensor", trlk, r=["lkT", "identbf", "psB"], w=["psB"], tag=f"trlk_g{g}")
        for c in range(2):
            op("vector", lambda e, c=c: e.tensor_copy(lk_nat[:, c, :],
                pB[:, 256 + c * 64:320 + c * 64].bitcast(BF16)), r=["psB"], w=[f"lkn{c}"], tag=f"evlk{c}_g{g}")
        def mmsim(e):
            e.matmul(pE[:, 0:256], qkkT[:, 0:P], qqT[:], start=True, stop=True)
            return e.matmul(pE[:, 256:512], qkkT[:, P:2 * P], qqT[:], start=True, stop=True)
        op("tensor", mmsim, r=["qqT", "qkkT", "psE"], w=["psE"], tag=f"mmsim_g{g}")
        for jc in range(2):
            op("vector", lambda e, jc=jc: e.tensor_tensor(out=u_at[:],
                in0=pE[:, jc * 256:(jc + 1) * 256], in1=bmt[:, jc, :], op=ALU.add),
               r=["psE", "bmt", "uat"], w=["uat"], tag=f"bmtadd{jc}_g{g}")
            op("scalar", lambda e, jc=jc: e.activation(out=u_at[:], in_=u_at[:],
                func=AF.Relu, scale=1.0 / GSZ), r=["uat"], w=["uat"], tag=f"relu{jc}_g{g}")
            op("vector", lambda e, jc=jc: e.tensor_tensor(out=attnT[:, jc, :],
                in0=u_at[:], in1=u_at[:], op=ALU.mult),
               r=["uat"], w=[f"attnT{jc}", "uat"], tag=f"sq{jc}_g{g}")

    def A1b(g):
        for c in range(2):
            for n in range(4):
                half = n % 2
                pctok = f"psC{half}"
                def mmv(e, c=c, n=n, half=half):
                    e.matmul(pC[:, half * 512:(half + 1) * 512], ones_bf[:],
                             bhv_bf[0:1, n * 512:(n + 1) * 512], start=True, stop=False)
                    last = None
                    for k in range(KD):
                        last = e.matmul(pC[:, half * 512:(half + 1) * 512],
                                        tT[:, k, c * P:(c + 1) * P],
                                        wh_bf[:, k, n * 512:(n + 1) * 512],
                                        start=False, stop=(k == KD - 1))
                    return last
                op("tensor", mmv, r=[f"tT{c}", "ones_bf", "bhv_bf"] + [f"wh{k}" for k in range(KD)] + [pctok],
                   w=[pctok], tag=f"mmv{c}{n}_g{g}")
                op("scalar", lambda e, c=c, n=n, half=half: e.activation(
                    out=v_bf[:, c, n * 512:(n + 1) * 512], in_=pC[:, half * 512:(half + 1) * 512],
                    func=AF.Silu, scale=1.0), r=[pctok], w=[f"v_{c}_{n}"], tag=f"silv{c}{n}_g{g}")

    def A2(g):
        for eix in range(16):
            bank = eix % 2
            sub = (eix // 2) % 2
            off = bank * 512 + sub * 256
            pdtok = f"psD{bank}"
            def mmg(e, eix=eix, off=off):
                last = None
                for k in range(KD):
                    last = e.matmul(pD[:, off:off + 256],
                                    wh_bf[:, k, HID + eix * P: HID + (eix + 1) * P],
                                    tT[:, k, :], start=(k == 0), stop=(k == KD - 1))
                return last
            op("tensor", mmg, r=["tT0", "tT1"] + [f"wh{k}" for k in range(KD)] + [pdtok],
               w=[pdtok], tag=f"mmg{eix}_g{g}")
            op("scalar", lambda e, eix=eix, off=off: e.activation(
                out=gateT[:, eix, :], in_=pD[:, off:off + 256],
                func=AF.Silu, bias=bhg_c[:, eix:eix + 1], scale=1.0),
               r=[pdtok, "bhg_c"], w=[f"gateT{eix}"], tag=f"silg{eix}_g{g}")

    def B(g, mid=None):
        def emit_mmql(eix):
            pr = (eix // 2) % 3
            sub = (eix % 2) * 256
            pq, pbtok = ((pB, "psB"), (pE, "psE"), (pA, "psA0"))[pr]
            def mmql(e, eix=eix, pq=pq, sub=sub):
                e.matmul(pq[:, sub:sub + 256], v_bf[:, 0, eix * P:(eix + 1) * P],
                         attnT[:, 0, :], start=True, stop=False)
                e.matmul(pq[:, sub:sub + 256], v_bf[:, 1, eix * P:(eix + 1) * P],
                         attnT[:, 1, :], start=False, stop=False)
                return e.matmul(pq[:, sub:sub + 256], S_bf[:, eix * P:(eix + 1) * P],
                                lqT[:], start=False, stop=True)
            rd = [f"v_0_{eix // 4}", f"v_1_{eix // 4}", "attnT0", "attnT1", "S_bf", "lqT", pbtok]
            op("tensor", mmql, r=rd, w=[pbtok], tag=f"mmql{eix}_g{g}")
            if eix % 2 == 1:
                op("vector", lambda e, eix=eix, pq=pq, pr=pr: e.tensor_tensor(
                    out=zT[:, pr, :], in0=pq[:, 0:512],
                    in1=gateT.ap()[:, eix - 1:eix + 1, :].rearrange("p a b -> p (a b)"),
                    op=ALU.mult),
                   r=[pbtok, f"gateT{eix - 1}", f"gateT{eix}"], w=[f"zT{pr}"], tag=f"zt{eix}_g{g}")
        def emit_wld(jj, gg=None):
            gg = g if gg is None else gg
            if gg > 0:
                def ldwo(e, jj=jj):
                    src = woutbf_hbm.ap().rearrange("(j p s) f -> j p (s f)", j=8, p=P)
                    return e.dma_start(
                        out=wout_win.ap()[:, (jj % 2) * 2:(jj % 2) * 2 + 2, :]
                            .rearrange("p a b -> p (a b)"),
                        in_=src[jj])
                qe, qn = (("scalar", "wout") if jj % 2 == 0 else ("gpsimd", "misc"))
                op(qe, ldwo, r=[f"wobf{2 * jj}", f"wobf{2 * jj + 1}"],
                   w=[f"wow{(2 * jj) % 4}", f"wow{(2 * jj + 1) % 4}"], dma=qn,
                   tag=f"ldwow{jj}_g{gg}")

        def emit_wout(eix):
            slot = (eix // 2) % 3
            zsub = (eix % 2) * 256
            for tc in range(2):
                pt = pC if tc == 0 else pD
                ptok = [f"psC{b}" for b in range(2)] if tc == 0 else [f"psD{b}" for b in range(2)]
                wsrc, wtok = wout_win.ap()[:, eix % 4, :], f"wow{eix % 4}"
                def mmwo(e, eix=eix, tc=tc, pt=pt, slot=slot, zsub=zsub, wsrc=wsrc):
                    e.matmul(pt[:, 0:512], zT[:, slot, zsub + tc * P:zsub + (tc + 1) * P],
                             wsrc[:, 0:512], start=(eix == 0), stop=(eix == 15))
                    last = e.matmul(pt[:, 512:1024], zT[:, slot, zsub + tc * P:zsub + (tc + 1) * P],
                                    wsrc[:, 512:1024], start=(eix == 0), stop=(eix == 15))
                    if eix == 0:
                        e.matmul(pt[:, 0:512], ones_bf[:], bout_bf[0:1, 0:512],
                                 start=False, stop=False)
                        last = e.matmul(pt[:, 512:1024], ones_bf[:], bout_bf[0:1, 512:1024],
                                        start=False, stop=False)
                    return last
                rd_extra = ["ones_bf", "bout_bf"] if eix == 0 else []
                op("tensor", mmwo, r=[f"zT{slot}", wtok] + ptok + rd_extra,
                   w=ptok, tag=f"mmwo{eix}t{tc}_g{g}")
        def emit_build(r):
            def ldo(e, r=r):
                return e.dma_start(out=xln_f32v[:], in_=wout_d.ap()[r * P:(r + 1) * P, :])
            op("scalar", ldo, r=[], w=["xlnbf0", "xlnbf1"], dma="wout", tag=f"ldWo{r}")
            def cst(e, r=r):
                return e.tensor_copy(wout_win[:, r % 4, :], xln_f32v[:])
            op("vector", cst, r=["xlnbf0"], w=[f"wow{r % 4}"], tag=f"cWo{r}")
            def sto(e, r=r):
                dst = woutbf_hbm.ap().rearrange("(j p s) f -> j p s f", j=8, p=P)
                return e.dma_start(out=dst[r // 2, :, r % 2, :],
                                   in_=wout_win[:, r % 4, :])
            op("scalar", sto, r=[f"wow{r % 4}"], w=[f"wobf{r}"], dma="w2", tag=f"stWo{r}")
        for th in DEF.pop(g - 1, []):
            th()
        if g + 1 < NG:
            ld_x(g + 1)
        steps = list(mid) if mid is not None else []
        si = 0
        if g == 0:
            emit_build(0)
            emit_build(1)
            emit_build(2)
        lag = 2
        for eix in range(lag):
            emit_mmql(eix)
        for eix in range(lag, 16):
            take = 0 if eix < 4 else max(0, min(len(steps) - si, (len(steps) * (eix - 3)) // 12 - si))
            for _ in range(take):
                steps[si](); si += 1
            emit_mmql(eix)
            emit_wout(eix - lag)
            if eix % 2 == 1 and 2 <= (eix + 1) // 2 <= 7:
                emit_wld((eix + 1) // 2)
            if g == 0 and eix + 1 < 16:
                emit_build(eix + 1)
        while si < len(steps):
            steps[si](); si += 1
        for eix in range(16 - lag, 16):
            emit_wout(eix)
        if g + 1 < NG:
            emit_wld(0, gg=g + 1)
            emit_wld(1, gg=g + 1)

    def C_pre(g):
        par, xg, yg, xtok, ytok = bufs(g)
        for n in range(4):
            dst = (pA[:, 0:512], "psA0") if n == 0 else \
                  (pA[:, 512:1024], "psA1") if n == 1 else \
                  (pE[:, 0:512], "psE") if n == 2 else (pB[:, 0:512], "psB")
            dap, dtok = dst
            def mmkv(e, n=n, dap=dap):
                e.matmul(dap, lk_nat[:, 0, :], v_bf[:, 0, n * 512:(n + 1) * 512], start=True, stop=False)
                return e.matmul(dap, lk_nat[:, 1, :], v_bf[:, 1, n * 512:(n + 1) * 512], start=False, stop=True)
            op("tensor", mmkv, r=["lkn0", "lkn1", f"v_0_{n}", f"v_1_{n}", dtok], w=[dtok], tag=f"mmkv{n}_g{g}")
            op("vector", lambda e, n=n, dap=dap: e.tensor_tensor(out=S[:, n * 512:(n + 1) * 512],
                in0=dap, in1=S[:, n * 512:(n + 1) * 512], op=ALU.add),
               r=[dtok, "S"], w=["S"], tag=f"Sup{n}_g{g}")


    def C_ao(g):
        par, xg, yg, xtok, ytok = bufs(g)
        for c in range(2):
            pt = pC if c == 0 else pD
            ptok = [f"psC{b}" for b in range(2)] if c == 0 else [f"psD{b}" for b in range(2)]
            op("vector", lambda e, c=c, pt=pt, yg=yg: e.tensor_tensor(out=ao[:, c * 1024:(c + 1) * 1024],
                in0=pt[:], in1=yg[:, c * 1024:(c + 1) * 1024], op=ALU.add),
               r=ptok + [ytok], w=[f"ao{c}"], tag=f"ao{c}_g{g}")
        emit_dbg("ao", ao[:, :], ["ao0", "ao1"])

    def S_cast(g):
        op("scalar", lambda e: e.activation(out=S_bf[:], in_=S[:], func=AF.Identity, scale=1.0),
           r=["S"], w=["S_bf"], tag=f"Sbf_g{g}")

    def C_mid1b(g):
        par, xg, yg, xtok, ytok = bufs(g)
        c3 = ln_site(3, g, lambda c, h: ao[:, c * 1024 + h * 512: c * 1024 + (h + 1) * 512],
                     None, per_chunk_toks=["ao0", "ao1"])
        c3m = [stat_cols(3, g % 2, c)["ag2"][:, 0:1] for c in range(2)]
        for c in range(2):
            r_, b_, tks = c3[c]
            m_ = c3m[c]
            op("vector", lambda e, c=c, r_=r_, m_=m_: e.tensor_scalar(
                out=t35[:], in0=ao[:, c * 1024:(c + 1) * 1024],
                scalar1=m_, scalar2=r_, op0=ALU.subtract, op1=ALU.mult),
               r=[f"ao{c}"] + tks, w=["t35x"], tag=f"t3a{c}_g{g}")
            op("vector", lambda e, c=c, xg=xg: e.tensor_tensor(out=xg[:, c * 1024:(c + 1) * 1024],
                in0=xg[:, c * 1024:(c + 1) * 1024], in1=t35[:], op=ALU.add),
               r=[xtok + f"c{c}", "t35x"], w=[xtok + f"c{c}"], tag=f"x2{c}_g{g}")
        emit_dbg("x2", xg[:, :], [xtok + "c0", xtok + "c1"])

    def C_mid2(g):
        par, xg, yg, xtok, ytok = bufs(g)
        c4 = ln_site(4, g, lambda c, h, xg=xg: xg[:, c * 1024 + h * 512: c * 1024 + (h + 1) * 512],
                     None, per_chunk_toks=[xtok + "c0", xtok + "c1"])
        c4m = [stat_cols(4, g % 2, c)["ag2"][:, 0:1] for c in range(2)]
        for c in range(2):
            r_, b_, tks = c4[c]
            m_ = c4m[c]
            op("vector", lambda e, c=c, r_=r_, m_=m_, xg=xg: e.tensor_scalar(
                out=xln_bf[:, c * 1024:(c + 1) * 1024], in0=xg[:, c * 1024:(c + 1) * 1024],
                scalar1=m_, scalar2=r_, op0=ALU.subtract, op1=ALU.mult),
               r=[xtok + f"c{c}"] + tks, w=[f"xlnbf{c}"], tag=f"xln{c}_g{g}")

    def C_ten(g):
        for c in range(2):
            def trx(e, c=c):
                last = None
                for k in range(KD):
                    last = e.transpose(
                        pA[:, c * 512:(c + 1) * 512].bitcast(BF16)[:, k * P:(k + 1) * P],
                        xln_bf[:, c * 1024 + k * P: c * 1024 + (k + 1) * P], ident_bf[:])
                return last
            op("tensor", trx, r=[f"xlnbf{c}", "identbf", f"psA{c}"], w=[f"psA{c}"], tag=f"trx{c}_g{g}")
            def evx(e, c=c):
                return e.activation(out=xlnT[:, :, c * P:(c + 1) * P],
                    in_=pA[:, c * 512:(c + 1) * 512].bitcast(BF16).rearrange("p (k q) -> p k q", k=KD),
                    func=AF.Identity, scale=1.0)
            op("scalar", evx, r=[f"psA{c}"], w=[f"xlnT{c}"], tag=f"evx{c}_g{g}")
        for c in range(2):
            pt = pC if c == 0 else pD
            ptok = [f"psC{b}" for b in range(2)] if c == 0 else [f"psD{b}" for b in range(2)]
            def mmf(e, c=c, pt=pt):
                last = None
                for k in range(KD):
                    e.matmul(pt[:, 0:512], xlnT[:, k, c * P:(c + 1) * P], swt_bf[:, k, 0:512],
                             start=(k == 0), stop=(k == KD - 1))
                    last = e.matmul(pt[:, 512:1024], xlnT[:, k, c * P:(c + 1) * P], swt_bf[:, k, 512:1024],
                                    start=(k == 0), stop=(k == KD - 1))
                return last
            op("tensor", mmf, r=[f"xlnT{c}"] + [f"swt{cc}" for cc in range(KD)] + ptok,
               w=ptok, tag=f"mmf{c}_g{g}")
            C_post_c(g, c)

    DEF = {}

    def C_post_c(g, c):
        par, xg, yg, xtok, ytok = bufs(g)
        if True:
            pt = pC if c == 0 else pD
            ptok = [f"psC{b}" for b in range(2)] if c == 0 else [f"psD{b}" for b in range(2)]
            sc = stat_cols(5, g % 2, c)
            tok = sc["tok"]
            op("vector", lambda e, sc=sc, pt=pt: e.bn_stats(sc["st12"][:, 0:6], pt[:, 0:512]),
               r=ptok, w=[tok + "a"], tag=f"bn5{c}a_g{g}")
            op("vector", lambda e, sc=sc, pt=pt: e.bn_stats(sc["st12"][:, 6:12], pt[:, 512:1024]),
               r=ptok, w=[tok + "b"], tag=f"bn5{c}b_g{g}")
            op("vector", lambda e, sc=sc: e.bn_aggr(sc["ag2"], sc["st12"]),
               r=[tok + "a", tok + "b"], w=[tok + "g"], tag=f"ag5{c}_g{g}")
            op("gpsimd", lambda e, sc=sc: e.tensor_tensor(out=sc["ve"], in0=sc["ag2"][:, 1:2],
                in1=epscol[:], op=ALU.add), r=[tok + "g", "epscol"], w=[tok + "v"], tag=f"ve5{c}_g{g}")
            op("gpsimd", lambda e, sc=sc: e.tensor_tensor(out=sc["rstd"], in0=sc["ve"],
                in1=neghalf[:], op=ALU.pow), r=[tok + "v", "neghalf"], w=[tok + "r"], tag=f"pow5{c}_g{g}")
            op("gpsimd", lambda e, sc=sc: e.tensor_scalar(out=sc["bcol"], in0=sc["ag2"][:, 0:1],
                scalar1=sc["rstd"], scalar2=-1.0, op0=ALU.mult, op1=ALU.mult),
               r=[tok + "g", tok + "r"], w=[tok + "bc"], tag=f"bc5{c}_g{g}")
            if g + 1 < NG and c == 1:
                dst, dtk = ao.ap()[:, 1024:2048], "ao1"
            else:
                dst, dtk = t35.ap()[:], "t35x"
            op("scalar", lambda e, sc=sc, pt=pt, dst=dst: e.activation(
                out=dst, in_=pt[:],
                func=AF.Identity, bias=sc["bcol"], scale=sc["rstd"]),
               r=ptok + [tok + "r", tok + "bc"], w=[dtk], tag=f"t5a{c}_g{g}")
            if g + 1 < NG:
                def x3d(c=c, xg=xg, xtok=xtok, g=g, dst=dst, dtk=dtk):
                    op("vector", lambda e, c=c, xg=xg, dst=dst: e.tensor_tensor(
                        out=ao[:, c * 1024:(c + 1) * 1024],
                        in0=xg[:, c * 1024:(c + 1) * 1024], in1=dst, op=ALU.add),
                       r=[xtok + f"c{c}", dtk] + (["ao0"] if c == 0 else []),
                       w=[f"ao{c}"], tag=f"x3{c}_g{g}")
                DEF.setdefault(g, []).append(x3d)
            else:
                op("vector", lambda e, c=c, xg=xg, dst=dst: e.tensor_tensor(
                    out=xg[:, c * 1024:(c + 1) * 1024],
                    in0=xg[:, c * 1024:(c + 1) * 1024], in1=dst, op=ALU.add),
                   r=[xtok + f"c{c}", dtk], w=[xtok + f"c{c}"], tag=f"x3{c}_g{g}")

    def C_post(g):
        par, xg, yg, xtok, ytok = bufs(g)
        if g + 1 < NG:
            def emit_stx(g=g):
                def stx(e, g=g):
                    return e.dma_start(out=out_d.ap()[g * GSZ:(g + 1) * GSZ, :].rearrange("(c p) f -> p c f", p=P),
                                       in_=ao.ap().rearrange("p (c f) -> p c f", c=2))
                op("gpsimd", stx, r=["ao0", "ao1"], dma="xout", tag=f"stx_g{g}")
            DEF.setdefault(g, []).append(emit_stx)
        else:
            def stx(e, g=g, xg=xg):
                return e.dma_start(out=out_d.ap()[g * GSZ:(g + 1) * GSZ, :].rearrange("(c p) f -> p c f", p=P),
                                   in_=xg.ap().rearrange("p (c f) -> p c f", c=2))
            op("gpsimd", stx, r=[xtok + "c0", xtok + "c1"], dma="xout", tag=f"stx_g{g}")


    op("gpsimd", lambda e: e.memset(neghalf[:], -0.5), w=["neghalf"], tag="ngh")
    op("gpsimd", lambda e: e.memset(epscol[:], EPS), w=["epscol"], tag="eps")
    op("gpsimd", lambda e: e.memset(ident[:], 0.0), w=["ident", "qkT_sil"], tag="id0")
    op("gpsimd", lambda e: e.affine_select(out=ident[:], in_=ident[:],
        compare_op=ALU.not_equal, fill=1.0, base=0, pattern=[[-1, P]],
        channel_multiplier=1), r=["ident"], w=["ident"], tag="id1")
    op("gpsimd", lambda e: e.memset(ident_bf[:], 0.0), w=["identbf"], tag="idb0")
    op("gpsimd", lambda e: e.affine_select(out=ident_bf[:], in_=ident_bf[:],
        compare_op=ALU.not_equal, fill=1.0, base=0, pattern=[[-1, P]],
        channel_multiplier=1), r=["identbf"], w=["identbf"], tag="idb1")
    op("gpsimd", lambda e: e.memset(anti[:], 0.0), w=["anti", "zT0"], tag="an0")
    op("gpsimd", lambda e: e.affine_select(out=anti[:], in_=anti[:],
        compare_op=ALU.not_equal, fill=1.0, base=-127, pattern=[[1, P]],
        channel_multiplier=1), r=["anti"], w=["anti"], tag="an1")
    for jc in range(2):
        op("gpsimd", lambda e, jc=jc: e.memset(masks[:, jc, :], 0.0),
           w=[f"mask{jc}", "xb1c0"], tag=f"mask0{jc}")
        op("gpsimd", lambda e, jc=jc: e.affine_select(out=masks[:, jc, :], in_=masks[:, jc, :],
            compare_op=ALU.is_ge, fill=NEG, base=-jc * P,
            pattern=[[1, GSZ]], channel_multiplier=-1),
           r=[f"mask{jc}"], w=[f"mask{jc}", "xb1c0"], tag=f"mask1{jc}")
    op("gpsimd", lambda e: e.memset(S[:], 0.0), w=["S"], tag="S0")
    op("gpsimd", lambda e: e.memset(S_bf[:], 0.0), w=["S_bf"], tag="Sbf0")

    def ld_x(g):
        def f(e, g=g):
            return e.dma_start(out=xb[g % 2].ap().rearrange("p (c f) -> p c f", c=2),
                in_=x_d.ap()[g * GSZ:(g + 1) * GSZ, :].rearrange("(c p) f -> p c f", p=P))
        op("sync", f, r=[], w=[f"xb{g % 2}c0", f"xb{g % 2}c1"], dma="xin", tag=f"ldx{g}")
    ld_x(0)

    def param_col(src_ap, nrows, dst_ap, dst_tok, nm):
        op("sync", lambda e: e.dma_start(out=prm_row[0:nrows, :], in_=src_ap),
           w=["prm", "wow0"], dma="wload", tag="ld_" + nm)
        op("tensor", lambda e: e.transpose(pB[:, 0:nrows], prm_row[0:nrows, :], ident[0:nrows, 0:nrows]),
           r=["prm", "ident", "psB"], w=["psB"], tag="tr_" + nm)
        op("vector", lambda e: e.tensor_copy(dst_ap, pB[:, 0:nrows]), r=["psB"], w=[dst_tok], tag="ev_" + nm)
    param_col(bh_d.ap()[HID:2 * HID].rearrange("(k p) -> k p", p=P), 16, bhg_c[:], "bhg_c", "bhg")
    param_col(bqk_d.ap().rearrange("(o p) -> o p", o=1), 1, bqk_c[:], "bqk_c", "bqk")
    param_col(qkw_d.ap(), 4, qkw_c[:], "qkw_c", "qkw")
    param_col(qkb_d.ap(), 4, qkb_c[:], "qkb_c", "qkb")
    op("vector", lambda e: e.tensor_scalar(out=w3s_c[:], in0=qkw_c[:, 3:4],
        scalar1=1.0 / GSZ, scalar2=None, op0=ALU.mult), r=["qkw_c"], w=["w3s_c"], tag="w3s")
    op("vector", lambda e: e.tensor_scalar(out=b3s_c[:], in0=qkb_c[:, 3:4],
        scalar1=1.0 / GSZ, scalar2=None, op0=ALU.mult), r=["qkb_c"], w=["b3s_c"], tag="b3s")

    op("sync", lambda e: e.dma_start(out=lnw_bc[:], in_=AP(lnw_d, 0, [[0, P], [1, DM]])),
       w=["lnw_bc"], dma="wload", tag="ld_lnwbc")
    op("sync", lambda e: e.dma_start(out=lnb_bc[:], in_=AP(lnb_d, 0, [[0, P], [1, DM]])),
       w=["lnb_bc"], dma="wload", tag="ld_lnbbc")
    op("gpsimd", lambda e: e.dma_start(out=bhv_bf[:], in_=bh_d.ap()[0:HID].rearrange("(o n) -> o n", o=1)),
       w=["bhv_bf"], dma="misc", tag="ld_bhv")
    op("gpsimd", lambda e: e.dma_start(out=bout_bf[:], in_=bout_d.ap().rearrange("(o n) -> o n", o=1)),
       w=["bout_bf"], dma="misc", tag="ld_bout")
    op("gpsimd", lambda e: e.memset(ones_bf[:], 1.0), w=["ones_bf"], tag="ones")

    op("sync", lambda e: e.dma_start(out=tbl_row[:], in_=tbl_d.ap().rearrange("(o n) -> o n", o=1)),
       w=["tbl_row", "wow0"], dma="wload", tag="ld_tbl")
    op("sync", lambda e: e.dma_start(out=oh_sb[:], in_=oh_d.ap()), w=["oh_sb", "v_0_0"], dma="wload", tag="ld_oh")
    op("tensor", lambda e: e.transpose(pB[0:NB, 28:29], tbl_row[0:1, :], ident[0:1, 0:1]),
       r=["tbl_row", "ident", "psB"], w=["psB"], tag="tr_tbl")
    op("vector", lambda e: e.tensor_copy(tbl_col[:], pB[0:NB, 28:29]), r=["psB"], w=["tbl_col"], tag="ev_tbl")
    op("tensor", lambda e: e.matmul(pE[0:1, 0:511], tbl_col[:], oh_sb[:], start=True, stop=True),
       r=["tbl_col", "oh_sb"], w=["psE"], tag="mm_F")
    op("vector", lambda e: e.tensor_copy(f_sb[:], pE[0:1, 0:511]), r=["psE"], w=["f_sb", "wow1"], tag="ev_F")
    op("gpsimd", lambda e: e.dma_start(out=f_hbm.ap()[0:511].rearrange("(o n) -> o n", o=1), in_=f_sb[:]),
       r=["f_sb"], w=["f_hbm"], dma="misc", tag="st_F")
    op("gpsimd", lambda e: e.dma_start(out=hank[:], in_=AP(f_hbm, 0, [[1, P], [1, 384]])),
       r=["f_hbm"], w=["hank", "gateT0", "gateT1", "gateT2"], dma="misc", tag="ld_hank")
    def mm_bias(e):
        e.matmul(pE[:, 0:256], anti[:], hank[:, 128:384], start=True, stop=True)
        return e.matmul(pE[:, 256:512], anti[:], hank[:, 0:256], start=True, stop=True)
    op("tensor", mm_bias, r=["anti", "hank", "psE"], w=["psE"], tag="mm_bias")
    for jc in range(2):
        op("vector", lambda e, jc=jc: e.tensor_tensor(out=bmt[:, jc, :], in0=pE[:, jc * 256:(jc + 1) * 256],
            in1=masks[:, jc, :], op=ALU.add), r=["psE", f"mask{jc}"], w=["bmt"], tag=f"bmt{jc}")

    op("sync", lambda e: e.dma_start(out=ao.ap()[:, 0:1024].rearrange("p (k q) -> p k q", k=KD),
        in_=wqk_d.ap().rearrange("(k p) q -> p k q", p=P)), w=["ao0", "ao1"], dma="wload", tag="ldWqk")
    op("scalar", lambda e: e.activation(out=wqk_bf.ap().rearrange("p k q -> p (k q)"),
        in_=ao[:, 0:1024], func=AF.Identity, scale=1.0), r=["ao0"], w=["wqk"], tag="cWqk")
    A_front(0)
    v_stage = v_bf.ap().rearrange("p a b -> p (a b)").bitcast(F32)
    gate_stage = gateT.ap().rearrange("p a b -> p (a b)").bitcast(F32)
    wh_chan = [("wload", "sync", ao, ["ao0", "ao1"]),
               ("w2", "scalar", gate_stage, ["gateT%d" % i for i in range(16)]),
               ("xin", "sync", v_stage, ["v_%d_%d" % (c, n) for c in range(2) for n in range(4)])]
    for k in range(KD):
        if True:
            q, qeng, stg, stok = wh_chan[k % 3]
            for h in range(2):
                def ldw(e, k=k, h=h, stg=stg):
                    return e.dma_start(out=stg[:],
                        in_=wh_d.ap().rearrange("(k p) m -> p k m", p=P)[:, k, h * 2048:(h + 1) * 2048])
                op(qeng, ldw, r=[], w=stok, dma=q, tag=f"ldWh{k}_{h}")
                def cst(e, k=k, h=h, stg=stg):
                    return e.tensor_copy(wh_bf[:, k, h * 2048:(h + 1) * 2048], stg[:])
                op("vector", cst, r=[stok[0]], w=[f"wh{k}"], tag=f"cWh{k}_{h}")
    A1a(0)
    A1b(0)
    A2(0)
    for c in range(KD):
        def ldf(e, c=c):
            return e.dma_start(out=t35[:],
                in_=wffn_d.ap()[c * P:(c + 1) * P, :])
        op("sync", ldf, r=[], w=["t35x"], dma="wload", tag=f"ldWf{c}")
        def sgn(e, c=c):
            return e.activation(out=xln_bf[:, 0:1024], in_=t35[:], func=AF.Sign, scale=1.0)
        op("scalar", sgn, r=["t35x"], w=["xlnbf0", "xlnbf1"], tag=f"sgnWf{c}")
        def trf(e, c=c):
            last = None
            for r in range(KD):
                last = e.transpose(pA[:, 0:1024].bitcast(BF16)[:, r * P:(r + 1) * P],
                                   xln_bf[:, r * P:(r + 1) * P], ident_bf[:])
            return last
        op("tensor", trf, r=["xlnbf0", "identbf", "psA0"], w=["psA0"], tag=f"trWf{c}")
        def evf(e, c=c):
            return e.tensor_copy(swt_bf[:, :, c * P:(c + 1) * P],
                                 pA[:, 0:512].bitcast(BF16).rearrange("p (r q) -> p r q", r=KD))
        op("vector", evf, r=["psA0"], w=[f"swt{c}"], tag=f"evWf{c}")
    for g in range(NG):
        B(g, mid=A_front_steps(g + 1) if g + 1 < NG else None)
        C_pre(g)
        C_ao(g)
        if g + 1 < NG:
            A1a(g + 1)
            A1b(g + 1)
            A2(g + 1)
        S_cast(g)
        C_mid1b(g)
        C_mid2(g)
        C_ten(g)
        C_post(g)

    with nc.Block() as block:
        semctx = ExitStack()
        sems = {n: semctx.enter_context(nc.semaphore(n)) for n in T.sem_names()}
        T.prepare(verbose=True)
        @block.tensor
        def _(t): T.emit_engine("tensor", t, sems)
        @block.vector
        def _(v): T.emit_engine("vector", v, sems)
        @block.scalar
        def _(s): T.emit_engine("scalar", s, sems)
        @block.gpsimd
        def _(gp): T.emit_engine("gpsimd", gp, sems)
        @block.sync
        def _(sy): T.emit_engine("sync", sy, sems, final_waiter=True)
        semctx.close()
    st.close()
    return nc


def make_inputs(inp, core, NG=16):
    b = core
    m = {
        "x": np.ascontiguousarray(np.asarray(inp["L0_attn_x"][b], np.float32)[:NG * GSZ]),
        "ln_w": np.asarray(inp["L0_attn_ln_w"], np.float32),
        "ln_b": np.asarray(inp["L0_attn_ln_b"], np.float32),
        "Wh": np.asarray(inp["L0_attn_Wh"], np.float32),
        "bh": np.asarray(inp["L0_attn_bh"], np.float32),
        "Wqk": np.asarray(inp["L0_attn_Wqk"], np.float32),
        "bqk": np.asarray(inp["L0_attn_bqk"], np.float32),
        "qk_w": np.asarray(inp["L0_attn_qk_w"], np.float32),
        "qk_b": np.asarray(inp["L0_attn_qk_b"], np.float32),
        "rel_table": np.asarray(inp["L0_attn_rel_table"], np.float32),
        "Wout": np.asarray(inp["L0_attn_Wout"], np.float32),
        "bout": np.asarray(inp["L0_attn_bout"], np.float32),
        "Wffn": np.asarray(inp["L0_ffn_weight"], np.float32),
        "onehot": onehot256(),
    }
    return m


_NC = 8

def kernel(**inputs):
    nc = build_program(NG=16)
    in_maps = [make_inputs(inputs, core) for core in range(_NC)]
    res = run_bass_kernel_spmd(nc, in_maps, core_ids=list(range(_NC)))
    return np.stack([res.results[c]["out"] for c in range(_NC)], axis=0)
